# version 02
# baseline (speedup 1.0000x reference)
import numpy as np
import ml_dtypes
from contextlib import ExitStack

import concourse.bass as bass
import concourse.mybir as mybir
from concourse.bass_utils import run_bass_kernel_spmd

F32 = mybir.dt.float32
BF16 = mybir.dt.bfloat16
AF = mybir.ActivationFunctionType
ALU = mybir.AluOpType
AX = mybir.AxisListType
bf = ml_dtypes.bfloat16

NCORE = 8
T = 8192
TC = 1024
D = 2048
KT = 16
H = 4
DK = 256
DV = 512
CH = 64
NCH = TC // CH
HB = 16
HD = 128
E = 16
FI = 1792
IM = FI // 128
GAM = [float(1.0 - 2.0 ** (-5 - h)) for h in range(H)]

N_SUB = 4
FFN_EXPERTS = 16
FFN_ROUTER = True



def _pretile(w):
    din, dout = w.shape
    k, m = din // 128, dout // 128
    t = w.reshape(k, 128, m, 128).transpose(2, 1, 0, 3).reshape(m, 128, din)
    return np.ascontiguousarray(t, dtype=np.float32)


def _manifest():
    g1 = [("wqT", 8, D), ("wkT", 8, D), ("wvT", 16, D), ("wgT", 16, D), ("woT", 16, D), ("wprojT", 16, D)]
    return g1


def _offsets(group):
    off, o = {}, 0
    for name, m, cols in group:
        off[name] = (o, m, cols)
        o += m * 128 * cols
    return off, o


def _rope_tables(core):
    i = np.arange(128, dtype=np.float64)
    inv_freq = (1.0 / 10000.0 ** ((2 * i) / 256.0)).astype(np.float32)
    pos = (core * TC + np.arange(TC)).astype(np.float32)
    ang = (pos[None, :] * inv_freq[:, None]).astype(np.float32)
    cos = np.cos(ang.astype(np.float64)).astype(np.float32)
    sin = np.sin(ang.astype(np.float64)).astype(np.float32)
    return np.stack([cos / 16.0, sin / 16.0, cos, sin])


def _decay_tables():
    tmod = np.arange(TC) % CH
    ging = np.stack([np.power(g, tmod + 1.0) for g in GAM]).astype(np.float32)
    ging = np.ascontiguousarray(np.broadcast_to(ging[:, None, :], (H, 128, TC))).astype(bf)
    p = np.arange(128) % CH
    gkinv = np.stack([np.power(g, -(p + 1.0)) for g in GAM], axis=1).astype(np.float32)
    cofp = np.arange(128) // CH
    gkm = np.zeros((128, NCH // 2 * H), dtype=np.float64)
    for tt in range(NCH // 2):
        for h in range(H):
            gkm[:, tt * H + h] = np.power(GAM[h], -64.0 * (2 * tt + cofp))
    gkm = gkm.astype(np.float32)
    j = np.arange(128) % CH
    ii = np.arange(CH)
    binm = (ii[None, :] >= j[:, None]).astype(bf)
    return ging, gkinv, gkm, binm


def _coef_table(core):
    c = np.zeros((NCORE, H), dtype=np.float64)
    for jj in range(core):
        for h in range(H):
            c[jj, h] = GAM[h] ** (1024.0 * (core - jj))
    c = c.reshape(-1).astype(np.float32)
    return np.ascontiguousarray(np.broadcast_to(c[None, :], (128, NCORE * H)))


def _tri_mask():
    i = np.arange(128)
    return np.where(i[:, None] <= i[None, :], np.float32(0.0), np.float32(-1e30)).astype(bf)


ENGS = ("pe", "dve", "act", "pool", "sp")
_BLOCK_ATTR = {"pe": "tensor", "dve": "vector", "act": "scalar", "pool": "gpsimd", "sp": "sync"}


class Prog:
    def __init__(self, nc, ctx):
        self.nc = nc
        self.ctx = ctx
        self.streams = {e: [] for e in ENGS}
        self.sems = {}
        self.cnt = {}
        self.waited = {e: {} for e in ENGS}
        self.phase = "p0"
        self.n_ops = 0

    def _sem(self, name):
        if name not in self.sems:
            self.sems[name] = self.ctx.enter_context(self.nc.semaphore(name))
            self.cnt[name] = 0
        return self.sems[name]

    def wait(self, eng, tok):
        s, v = tok
        if self.waited[eng].get(s, 0) >= v:
            return
        self.waited[eng][s] = v
        self._sem(s)
        self.streams[eng].append(("w", s, v))

    def op(self, eng, fn, deps=(), dma=False, ch=None, sem_name=None):
        byname = {}
        for d in deps:
            if d is None:
                continue
            s, v = d
            if byname.get(s, 0) < v:
                byname[s] = v
        for s, v in byname.items():
            self.wait(eng, (s, v))
        if sem_name is not None:
            sname = sem_name
        elif dma:
            sname = f"D{eng}_{self.phase}_{ch or 'x'}"
        else:
            sname = f"S{eng}_{self.phase}"
        self._sem(sname)
        step = 16 if dma else 1
        self.cnt[sname] += step
        self.streams[eng].append(("i", fn, sname, step))
        self.n_ops += 1
        return (sname, self.cnt[sname])

    def dma(self, eng, dst, src, deps=(), ch=None, **kw):
        return self.op(eng, (lambda E, dst=dst, src=src, kw=kw: E.dma_start(dst, src, **kw)),
                       deps, dma=True, ch=ch)

    _ncoll = 0

    def coll(self, fn, deps=()):
        Prog._ncoll += 1
        return self.op("pool", fn, deps, sem_name=f"CC{Prog._ncoll}")

    def finalize(self):
        blk = self.ctx.enter_context(self.nc.Block())
        for ename in ENGS:
            stream = self.streams[ename]
            sems = self.sems

            def run(EO, stream=stream, sems=sems):
                for item in stream:
                    if item[0] == "w":
                        EO.wait_ge(sems[item[1]], item[2])
                    else:
                        item[1](EO).then_inc(sems[item[2]], item[3])

            getattr(blk, _BLOCK_ATTR[ename])(run)



def build_nc(n_sub=4):
    nc = bass.Bass("TRN2", target_bir_lowering=False)
    g1 = _manifest()
    off1, n1 = _offsets(g1)
    assert n1 % (NCORE * 1024) == 0
    sh1 = n1 // NCORE
    R2A = E * 2 * IM * 128
    R2B = E * 16 * 128

    xT = nc.dram_tensor("xT", [D, TC], F32, kind="ExternalInput").ap()
    wsh1 = nc.dram_tensor("wsh1", [sh1 // 1024, 1024], F32, kind="ExternalInput").ap()
    wsh2a = nc.dram_tensor("wsh2a", [R2A // NCORE, D], F32, kind="ExternalInput").ap()
    wsh2b = nc.dram_tensor("wsh2b", [R2B // NCORE, FI], F32, kind="ExternalInput").ap()
    wqkvT_i = nc.dram_tensor("wqkvT", [6, 128, D], F32, kind="ExternalInput").ap()
    bqkv_i = nc.dram_tensor("bqkv", [128, 6], F32, kind="ExternalInput").ap()
    bproj_i = nc.dram_tensor("bproj", [128, 16], F32, kind="ExternalInput").ap()
    rwT_i = nc.dram_tensor("rwT", [D, 16], F32, kind="ExternalInput").ap()
    ebias_i = nc.dram_tensor("ebias", [128, 16], F32, kind="ExternalInput").ap()
    gnormw_i = nc.dram_tensor("gnormw", [128, DV], F32, kind="ExternalInput").ap()
    ropet_i = nc.dram_tensor("ropet", [4, 128, TC], F32, kind="ExternalInput").ap()
    ging_i = nc.dram_tensor("ging", [H, 128, TC], BF16, kind="ExternalInput").ap()
    gkinv_i = nc.dram_tensor("gkinv", [128, H], F32, kind="ExternalInput").ap()
    gkm_i = nc.dram_tensor("gkm", [128, NCH // 2 * H], F32, kind="ExternalInput").ap()
    coef_i = nc.dram_tensor("coef", [128, NCORE * H], F32, kind="ExternalInput").ap()
    binm_i = nc.dram_tensor("binm", [128, CH], BF16, kind="ExternalInput").ap()
    tri_i = nc.dram_tensor("tri", [128, 128], BF16, kind="ExternalInput").ap()
    identb_i = nc.dram_tensor("identb", [128, 128], BF16, kind="ExternalInput").ap()
    identf_i = nc.dram_tensor("identf", [128, 128], F32, kind="ExternalInput").ap()
    onesf_i = nc.dram_tensor("onesf", [128, 1], F32, kind="ExternalInput").ap()
    onesrowf_i = nc.dram_tensor("onesrowf", [1, 128], F32, kind="ExternalInput").ap()
    cid_i = nc.dram_tensor("cid", [1, 1], mybir.dt.int32, kind="ExternalInput").ap()
    out_t = nc.dram_tensor("out", [D, TC], F32, kind="ExternalOutput").ap()

    cc1_in = nc.dram_tensor("cc1_in", [sh1 // 1024, 1024], BF16)
    cc1_out = nc.dram_tensor("cc1_out", [n1 // 1024, 1024], BF16)
    cc2a_in = nc.dram_tensor("cc2a_in", [R2A // NCORE, D], BF16)
    cc2a_out = nc.dram_tensor("cc2a_out", [R2A, D], BF16)
    cc2b_in = nc.dram_tensor("cc2b_in", [R2B // NCORE, FI], BF16)
    cc2b_out = nc.dram_tensor("cc2b_out", [R2B, FI], BF16)
    ccm_in = nc.dram_tensor("ccm_in", [1024, DV], F32)
    ccm_out = nc.dram_tensor("ccm_out", [NCORE * 1024, DV], F32)
    ccxn_in = nc.dram_tensor("ccxn_in", [D, TC], BF16)
    ccxn_out = nc.dram_tensor("ccxn_out", [NCORE * D, TC], BF16)
    ccy_in = nc.dram_tensor("ccy_in", [2 * HD, T], BF16)
    ccy_out = nc.dram_tensor("ccy_out", [D, T], BF16)
    qs_d = nc.dram_tensor("qs_d", [NCH // 2, 128, 1024], BF16)
    vt_d = nc.dram_tensor("vt_d", [NCH // 2, 128, 2048], BF16)
    qb_d = nc.dram_tensor("qb_d", [64, 128, 256], BF16)
    wqkv_bf = nc.dram_tensor("wqkv_bf", [6 * 128, D], BF16)

    def slab1(name, m):
        o, nm, cols = off1[name]
        oo = o + m * 128 * cols
        return cc1_out[oo // 1024:(oo + 128 * cols) // 1024, :].rearrange("(p r) c -> p (r c)", p=128)

    def slab2g(e, m):
        r0 = (e * 2 * IM + m) * 128
        return cc2a_out[r0:r0 + 128, :]

    def slab2u(e, m):
        r0 = (e * 2 * IM + IM + m) * 128
        return cc2a_out[r0:r0 + 128, :]

    def slab2d(e, m):
        r0 = (e * 16 + m) * 128
        return cc2b_out[r0:r0 + 128, :]

    with ExitStack() as ctx:
        en = ctx.enter_context
        P = Prog(nc, ctx)

        af = en(nc.sbuf_tensor("s_af", [128, 18432], F32))
        ab = en(nc.sbuf_tensor("s_ab", [128, 62464], BF16))
        sm = en(nc.sbuf_tensor("s_sm", [128, 1024], F32))
        sm2 = en(nc.sbuf_tensor("s_sm2", [128, 1024], F32))
        smb = en(nc.sbuf_tensor("s_smb", [128, 256], F32))
        cst = en(nc.sbuf_tensor("s_cst", [128, 896], F32))
        idb = en(nc.sbuf_tensor("s_idb", [128, 128], BF16))
        cid_sb = en(nc.sbuf_tensor("s_cid", [1, 1], mybir.dt.int32))
        idf = en(nc.sbuf_tensor("s_idf", [128, 128], F32))
        onesf = en(nc.sbuf_tensor("s_onesf", [128, 1], F32))
        onesrf = en(nc.sbuf_tensor("s_onesrf", [1, 128], F32))

        xsb = af[:, 0:16384]
        ropetab = af[:, 0:4096]
        mjbuf = af[:, 4096:8192]
        ssb = af[:, 8192:12288]
        xtl = [af[:, 0:1024], af[:, 2048:3072]]
        xto = [af[:, 16384:17408], af[:, 17408:18432]]
        sq0 = af[:, 16384:17408]
        sq1 = af[:, 17408:18432]
        oacc = [sm2[:, 0:129], sm2[:, 256:385]]

        xn = ab[:, 0:16384]
        og = ab[:, 0:16384]
        gsilu = ab[:, 16384:32768]
        ktok = ab[:, 32768:40960]
        qstage = ab[:, 40960:43008]
        kstage = ab[:, 43008:45056]
        vstage = ab[:, 45056:53248]
        ging_s = ab[:, 53248:57344]
        wsl = [ab[:, 57344:59392], ab[:, 59392:61440]]
        k2b = [ab[:, 61440:61696], ab[:, 61696:61952]]
        sbf = ab[:, 40960:45056]
        kfsc = [ab[:, 45056:45184], ab[:, 45184:45312]]
        atsc = ab[:, 45312:45440]
        obsc = [ab[:, 45440:45952], ab[:, 45952:46464]]
        binm_s = ab[:, 61952:62016]
        qtile = [ab[:, 46592:47616], ab[:, 47616:48640]]
        vtile = [ab[:, 48640:50688], ab[:, 50688:52736]]
        fact = ab[:, 16384:30720]
        fgsl = [ab[:, 40960:43008], ab[:, 43008:45056]]
        fusl = [ab[:, 45056:47104], ab[:, 47104:49152]]
        fdsl = [ab[:, 49152:50944], ab[:, 50944:52736]]
        fgate = ab[:, 52736:53760]
        fub = ab[:, 53760:54784]
        fwbc = ab[:, 54784:55808]
        kb_sb = ab[:, 0:16384]
        vb_sb = ab[:, 16384:32896]
        wqkv_sb = ab[:, 32896:45184]
        xnb = [ab[:, 45184:49280], ab[:, 49280:53376]]
        pP = [ab[:, 53376:54400], ab[:, 54400:55424]]
        pTe = [ab[:, 55424:56448], ab[:, 56448:57472]]
        qbt = [ab[:, 57472:57728], ab[:, 57728:57984]]
        ytmp = ab[:, 57984:58112]
        ystage = ab[:, 58112:59136]
        tri_s = ab[:, 59136:59264]
        vsc_b = ab[:, 59264:59776]

        psA = en(nc.psum_tensor("psA", [128, 1024], F32))
        psB = en(nc.psum_tensor("psB", [128, 1024], F32))
        psC = en(nc.psum_tensor("psC", [128, 1024], F32))
        psT = en(nc.psum_tensor("psT", [128, 1024], BF16))
        psE = en(nc.psum_tensor("psE", [128, 512], F32))

        NIL = ("nil", 0)

        P.phase = "p0"
        P.dma("sp", idb[:], identb_i, ch="cst")
        t_cid = P.dma("sp", cid_sb[:], cid_i, ch="cid")
        P.dma("sp", idf[:], identf_i, ch="cst")
        P.dma("sp", onesf[:], onesf_i, ch="cst")
        P.dma("sp", onesrf[:], onesrowf_i, ch="cst")
        t_cst = []
        gkinv_c = cst[:, 0:H]
        gkm_c = cst[:, 16:16 + NCH // 2 * H]
        coef_c = cst[:, 64:64 + NCORE * H]
        gnormw_c = cst[:, 128:128 + DV]
        ebias_c = cst[:, 768:768 + 16]
        bqkv_c = cst[:, 800:800 + 6]
        bproj_c = cst[:, 832:832 + 16]
        bqs_c = cst[:, 864:864 + 2]
        t_cst.append(P.dma("sp", gkinv_c, gkinv_i, ch="cst"))
        t_cst.append(P.dma("sp", gkm_c, gkm_i, ch="cst"))
        t_cst.append(P.dma("sp", coef_c, coef_i, ch="cst"))
        t_cst.append(P.dma("sp", gnormw_c, gnormw_i, ch="cst"))
        t_cst.append(P.dma("sp", ebias_c, ebias_i, ch="cst"))
        t_cst.append(P.dma("sp", bqkv_c, bqkv_i, ch="cst"))
        t_cst.append(P.dma("sp", bproj_c, bproj_i, ch="cst"))
        t_cstF = t_cst[-1]
        t_id = t_idf = t_ones = t_onesr = t_cstF
        t_cst = [t_cstF]

        def cast_blob(tag, wsh, cc_in, rows):
            f32st = [af[:, 4096:6144], af[:, 6144:8192]]
            bfst = [ab[:, 16384:18432], ab[:, 18432:20480]]
            tprev = [NIL, NIL]
            toks = [NIL, NIL]
            nchunk = rows // 256
            for i in range(nchunk):
                b = i % 2
                src = wsh[i * 256:(i + 1) * 256, :].rearrange("(p r) c -> p (r c)", p=128)
                tl = P.dma("pool", f32st[b][:, 0:2048], src, deps=[tprev[b]], ch=f"{tag}l{b}")
                tc1 = P.op("dve", (lambda E, b=b: E.tensor_copy(bfst[b][:, 0:768], f32st[b][:, 0:768])), deps=[tl])
                tc2 = P.op("act", (lambda E, b=b: E.activation(bfst[b][:, 768:1536], f32st[b][:, 768:1536], AF.Identity)), deps=[tl])
                tc3 = P.op("pool", (lambda E, b=b: E.tensor_copy(bfst[b][:, 1536:2048], f32st[b][:, 1536:2048])), deps=[tl])
                dst = cc_in[i * 256:(i + 1) * 256, :].rearrange("(p r) c -> p (r c)", p=128)
                ts_ = P.dma("pool", dst, bfst[b][:, 0:2048], deps=[tc1, tc2, tc3], ch=f"{tag}s{b}")
                tprev[b] = ts_
                toks[b] = ts_
            return [t for t in toks if t is not NIL]

        rwb = ab[:, 62016:62272]
        t_rw1 = P.dma("sp", af[:, 8192:8448].rearrange("p (k e) -> p k e", k=16),
                      rwT_i.rearrange("(k p) e -> p k e", p=128), ch="rw")
        t_rwb = P.op("dve", (lambda E: E.tensor_copy(rwb[:], af[:, 8192:8448])), deps=[t_rw1])

        cast1 = cast_blob("c1", wsh1, cc1_in, sh1 // 1024)
        t_ag1 = P.coll((lambda E: E.collective_compute(
            "AllGather", ALU.bypass, replica_groups=[list(range(NCORE))],
            ins=[cc1_in.ap().opt()], outs=[cc1_out.ap().opt()])), deps=cast1)

        def cast_wide(tag, wsh, cc_in, nrows, width, dep0=()):
            f32st = [af[:, 4096:6144], af[:, 6144:8192]]
            bfst = [ab[:, 16384:18432], ab[:, 18432:20480]]
            tprev = [list(dep0) or [NIL], list(dep0) or [NIL]]
            toks = [NIL, NIL]
            c1 = (width // 3) // 2 * 2
            c2 = 2 * c1
            for j in range(nrows // 128):
                b = j % 2
                tl = P.dma("pool", f32st[b][:, 0:width], wsh[j * 128:(j + 1) * 128, :],
                           deps=tprev[b], ch=f"{tag}l{b}")
                tc1 = P.op("dve", (lambda E, b=b, c1=c1: E.tensor_copy(bfst[b][:, 0:c1], f32st[b][:, 0:c1])), deps=[tl])
                tc2 = P.op("act", (lambda E, b=b, c1=c1, c2=c2: E.activation(bfst[b][:, c1:c2], f32st[b][:, c1:c2], AF.Identity)), deps=[tl])
                tc3 = P.op("pool", (lambda E, b=b, c2=c2, width=width: E.tensor_copy(bfst[b][:, c2:width], f32st[b][:, c2:width])), deps=[tl])
                ts_ = P.dma("pool", cc_in[j * 128:(j + 1) * 128, :], bfst[b][:, 0:width],
                            deps=[tc1, tc2, tc3], ch=f"{tag}s{b}")
                tprev[b] = [ts_]
                toks[b] = ts_
            return [t for t in toks if t is not NIL]

        cast2a = cast_wide("c2a", wsh2a, cc2a_in, R2A // NCORE, D, dep0=cast1)
        t_ag2a = P.coll((lambda E: E.collective_compute(
            "AllGather", ALU.bypass, replica_groups=[list(range(NCORE))],
            ins=[cc2a_in.ap().opt()], outs=[cc2a_out.ap().opt()])), deps=cast2a)
        cast2b = cast_wide("c2b", wsh2b, cc2b_in, R2B // NCORE, FI, dep0=cast2a)
        t_ag2b = P.coll((lambda E: E.collective_compute(
            "AllGather", ALU.bypass, replica_groups=[list(range(NCORE))],
            ins=[cc2b_in.ap().opt()], outs=[cc2b_out.ap().opt()])), deps=cast2b)
        castq = cast_wide("cq", wqkvT_i.rearrange("s p c -> (s p) c"), wqkv_bf, 6 * 128, D,
                          dep0=cast2b)
        cast2 = cast2a + cast2b + castq

        def layernorm(phase, src_tile_fn, out_buf, n_tiles=KT, sq_war=None):
            P.phase = phase
            sqb = [sq0, sq1]
            tsq = [sq_war or NIL, sq_war or NIL]
            tmm = []
            for k in range(n_tiles):
                xap, tx, set_reader = src_tile_fn(k)
                b = k % 2
                tq = P.op("act", (lambda E, xap=xap, b=b: E.activation(sqb[b][:], xap, AF.Square)),
                          deps=[tx, tsq[b]])
                tm2 = None
                for n in range(2):
                    tgt = [(psE[0:1, :], psE[32:33, :]), (psE[64:65, :], psC[0:1, 0:512])][n]
                    tm1 = P.op("pe", (lambda E, xap=xap, n=n, k=k, tgt=tgt: E.matmul(
                        tgt[0], onesf[:], xap[:, n * 512:(n + 1) * 512],
                        start=(k == 0), stop=(k == n_tiles - 1))), deps=[tx, t_ones])
                    tm2 = P.op("pe", (lambda E, b=b, n=n, k=k, tgt=tgt: E.matmul(
                        tgt[1], onesf[:], sqb[b][:, n * 512:(n + 1) * 512],
                        start=(k == 0), stop=(k == n_tiles - 1))), deps=[tq])
                    if k == n_tiles - 1:
                        tmm += [tm1, tm2]
                set_reader(tm2)
                tsq[b] = tm2
            te = P.op("dve", (lambda E: E.memset(smb[0:1, 0:1], 1e-5)))
            tstat = []
            srcs = [(psE[0:1, :], psE[32:33, :]), (psE[64:65, :], psC[0:1, 0:512])]
            for n in range(2):
                sx, sq_ = srcs[n]
                c0 = n * 512
                t1 = P.op("dve", (lambda E, sx=sx, c0=c0: E.tensor_scalar(
                    sm[0:1, c0:c0 + 512], sx, 1.0 / D, None, ALU.mult)), deps=tmm)
                t2 = P.op("dve", (lambda E, sq_=sq_, c0=c0: E.tensor_scalar(
                    sm2[0:1, c0:c0 + 512], sq_, 1.0 / D, None, ALU.mult)), deps=tmm)
                t3 = P.op("dve", (lambda E, c0=c0: E.tensor_tensor(
                    sq0[0:1, c0:c0 + 512], sm[0:1, c0:c0 + 512], sm[0:1, c0:c0 + 512], ALU.mult)),
                    deps=[t1] + [tsq[0], tsq[1]])
                t4 = P.op("dve", (lambda E, c0=c0: E.tensor_tensor(
                    sm2[0:1, c0:c0 + 512], sm2[0:1, c0:c0 + 512], sq0[0:1, c0:c0 + 512], ALU.subtract)), deps=[t2, t3])
                t5 = P.op("act", (lambda E, c0=c0: E.activation(
                    sm2[0:1, c0:c0 + 512], sm2[0:1, c0:c0 + 512], AF.Sqrt, bias=smb[0:1, 0:1])), deps=[t4, te])
                t6 = P.op("dve", (lambda E, c0=c0: E.reciprocal(
                    sq0[0:1, c0:c0 + 512], sm2[0:1, c0:c0 + 512])), deps=[t5, t3])
                t7 = P.op("dve", (lambda E, c0=c0: E.tensor_tensor(
                    sm2[0:1, c0:c0 + 512], sm[0:1, c0:c0 + 512], sq0[0:1, c0:c0 + 512], ALU.mult)), deps=[t6, t5])
                tstat += [t6, t7]
            tbc = []
            for n in range(2):
                tb1 = P.op("pe", (lambda E, n=n: E.matmul(
                    psA[:, n * 512:(n + 1) * 512], onesrf[:], sq0[0:1, n * 512:(n + 1) * 512],
                    start=True, stop=True)), deps=tstat + [t_onesr] + tmm)
                tb2 = P.op("pe", (lambda E, n=n: E.matmul(
                    psB[:, n * 512:(n + 1) * 512], onesrf[:], sm2[0:1, n * 512:(n + 1) * 512],
                    start=True, stop=True)), deps=tstat + [t_onesr])
                tbc += [tb1, tb2]
            tw = []
            tscr = [NIL, NIL]
            for k in range(n_tiles):
                xap, tx, set_reader = src_tile_fn(k)
                b = k % 2
                sc = sqb[b]
                ta = P.op("dve", (lambda E, xap=xap, sc=sc: E.tensor_tensor(
                    sc[:], xap, psA[:], ALU.mult)), deps=[tx, tscr[b]] + tbc)
                tb_ = P.op("dve", (lambda E, sc=sc, k=k: E.tensor_tensor(
                    out_buf[:, k * 1024:(k + 1) * 1024], sc[:], psB[:], ALU.subtract)), deps=[ta])
                set_reader(ta)
                tscr[b] = tb_
                tw.append(tb_)
            return tw, tbc

        def xdram_stream(bufs, tag, init_dep=None):
            readers = [init_dep or NIL, init_dep or NIL]

            def tile(k):
                b = k % 2
                buf = bufs[b]
                tl = P.dma("act", buf[:], xT[k * 128:(k + 1) * 128, :], deps=[readers[b]], ch=f"{tag}{b}")

                def set_reader(tok, b=b):
                    readers[b] = tok
                return buf[:], tl, set_reader

            return tile

        ln1_src = xdram_stream(xtl, "xl")
        t_xn1, _ = layernorm("ln1", (lambda k: ln1_src(k)), xn)

        def attn_a(t_cast_sync):
            P.phase = "aa"
            t_rt = [P.dma("sp", ropetab[:, i * 1024:(i + 1) * 1024], ropet_i[i],
                          deps=[t_ag1, t_xn1[-1]], ch="rt") for i in range(4)]
            t_rt = [t_rt[-1]]
            cosq, sinq = ropetab[:, 0:1024], ropetab[:, 1024:2048]
            cosk, sink = ropetab[:, 2048:3072], ropetab[:, 3072:4096]
            t_gi = [P.dma("sp", ging_s[:, h * 1024:(h + 1) * 1024], ging_i[h], ch="gi") for h in range(H)]
            t_gi = [t_gi[-1]]
            t_bm = P.dma("sp", binm_s[:], binm_i, ch="bm")

            ps_free = {"A": NIL, "B": NIL}
            wstate = {"i": 0, "rd": [NIL, NIL]}

            def wload(name, m, extra=()):
                b = wstate["i"] % 2
                tl = P.dma("sp", wsl[b][:], slab1(name, m), deps=[wstate["rd"][b], t_ag1] + list(extra), ch=f"w{b}")
                wstate["i"] += 1
                return wsl[b], tl, b

            def proj_full(name, m, ps, ps_key):
                wb, tw, b = wload(name, m)
                tm = None
                for n in range(2):
                    for k in range(KT):
                        deps = [tw] + (t_xn1 if (n == 0 and k == 0) else []) \
                            + ([ps_free[ps_key]] if (n == 0 and k == 0) else [])
                        tm = P.op("pe", (lambda E, wb=wb, k=k, n=n, ps=ps: E.matmul(
                            ps[:, n * 512:(n + 1) * 512], wb[:, k * 128:(k + 1) * 128],
                            xn[:, k * 1024 + n * 512:k * 1024 + (n + 1) * 512],
                            start=(k == 0), stop=(k == KT - 1))), deps=deps)
                wstate["rd"][b] = tm
                return tm

            s0, s1 = sq0, sq1
            s2, s3 = af[:, 8192:9216], af[:, 10240:11264]

            tko = {}
            t_ktok_all = []
            tpsT_free = NIL
            tks_free = [NIL, NIL]
            for h in range(H):
                tm1 = proj_full("wkT", 2 * h, psA, "A")
                tm2 = proj_full("wkT", 2 * h + 1, psB, "B")
                ta = P.op("dve", (lambda E: E.tensor_tensor(s0[:], psA[:], cosk, ALU.mult)), deps=[tm1] + t_rt)
                tb = P.op("dve", (lambda E: E.tensor_tensor(s1[:], psB[:], sink, ALU.mult)), deps=[tm2] + t_rt)
                tr1 = P.op("dve", (lambda E: E.tensor_tensor(kstage[:, 0:1024], s0[:], s1[:], ALU.subtract)),
                           deps=[ta, tb, tks_free[0]])
                tc_ = P.op("dve", (lambda E: E.tensor_tensor(s2[:], psB[:], cosk, ALU.mult)), deps=[tm2])
                td = P.op("dve", (lambda E: E.tensor_tensor(s3[:], psA[:], sink, ALU.mult)), deps=[tm1])
                tr2 = P.op("dve", (lambda E: E.tensor_tensor(kstage[:, 1024:2048], s2[:], s3[:], ALU.add)),
                           deps=[tc_, td, tks_free[1]])
                ps_free["A"] = td
                ps_free["B"] = tc_
                lastev = NIL
                for half, trr in ((0, tr1), (1, tr2)):
                    dt = 2 * h + half
                    for tt in range(8):
                        ttr = P.op("pe", (lambda E, half=half, tt=tt: E.transpose(
                            psT[:, (tt % 4) * 128:((tt % 4) + 1) * 128],
                            kstage[:, half * 1024 + tt * 128:half * 1024 + (tt + 1) * 128], idb[:])),
                            deps=[trr, t_id, tpsT_free])
                        tev = P.op("dve", (lambda E, tt=tt, dt=dt, h=h: E.tensor_scalar(
                            ktok[:, tt * 1024 + dt * 128:tt * 1024 + (dt + 1) * 128],
                            psT[:, (tt % 4) * 128:((tt % 4) + 1) * 128],
                            gkinv_c[:, h:h + 1], None, ALU.mult)), deps=[ttr] + t_cst)
                        tpsT_free = tev
                        t_ktok_all.append(tev)
                        lastev = tev
                    if half == 0:
                        tks_free[0] = lastev
                    else:
                        tks_free[1] = lastev
            t_ms0 = P.op("dve", (lambda E: E.memset(af[:, 12288:16384], 0.0)))
            psE_last = NIL
            msb = af[:, 12288:16384]
            t_msb = {(h, dt): t_ms0 for h in range(H) for dt in range(2)}
            t_vspill = {}
            vst_rd = NIL
            vev_free = [NIL, NIL]
            psV_free = [NIL, NIL, NIL, NIL]
            psV = [(psA, 0), (psA, 512), (psB, 0), (psB, 512)]
            k2rd = [NIL, NIL]
            t_ms_done = []
            for n in range(2):
                tvw = {}
                for m in range(16):
                    wb, twv, b = wload("wvT", m)
                    q = m % 4
                    ps, c0 = psV[q]
                    tm = None
                    for k in range(KT):
                        deps = [twv] + ([psV_free[q]] if k == 0 else [])
                        tm = P.op("pe", (lambda E, wb=wb, k=k, ps=ps, c0=c0, n=n: E.matmul(
                            ps[:, c0:c0 + 512], wb[:, k * 128:(k + 1) * 128],
                            xn[:, k * 1024 + n * 512:k * 1024 + (n + 1) * 512],
                            start=(k == 0), stop=(k == KT - 1))), deps=deps)
                    wstate["rd"][b] = tm
                    vb2 = m % 2
                    tev = P.op("act", (lambda E, ps=ps, c0=c0, vb2=vb2: E.activation(
                        kstage[:, vb2 * 512:vb2 * 512 + 512], ps[:, c0:c0 + 512], AF.Identity)),
                        deps=[tm, vev_free[vb2]])
                    psV_free[q] = tev
                    for j in range(4):
                        ttr = P.op("pe", (lambda E, j=j, vb2=vb2: E.transpose(
                            psT[:, 512 + j * 128:640 + j * 128],
                            kstage[:, vb2 * 512 + j * 128:vb2 * 512 + (j + 1) * 128], idb[:])),
                            deps=[tev, t_id, tpsT_free])
                        tvv = P.op("dve", (lambda E, j=j, m=m: E.tensor_copy(
                            vstage[:, j * 2048 + m * 128:j * 2048 + (m + 1) * 128],
                            psT[:, 512 + j * 128:640 + j * 128])), deps=[ttr, vst_rd])
                        tpsT_free = tvv
                        tvw[(j, m)] = tvv
                    vev_free[vb2] = tvw[(3, m)]
                for j in range(4):
                    tt = n * 4 + j
                    deps_v = [tvw[(j, m)] for m in range(16)]
                    for h in range(H):
                        b2 = (tt * H + h) % 2
                        tk2 = P.op("dve", (lambda E, tt=tt, h=h, b2=b2: E.tensor_scalar(
                            k2b[b2][:, 0:256], ktok[:, tt * 1024 + 2 * h * 128:tt * 1024 + (2 * h + 2) * 128],
                            gkm_c[:, tt * H + h:tt * H + h + 1], None, ALU.mult)),
                            deps=[k2rd[b2]] + t_ktok_all + t_cst)
                        for dt in range(2):
                            tmm_ = P.op("pe", (lambda E, b2=b2, dt=dt, j=j, h=h: E.matmul(
                                psE[:, 0:512], k2b[b2][:, dt * 128:(dt + 1) * 128],
                                vstage[:, j * 2048 + h * 512:j * 2048 + (h + 1) * 512],
                                start=True, stop=True)),
                                deps=[tk2, psE_last] + deps_v + [t_msb[(h, dt)]])
                            tad = P.op("dve", (lambda E, h=h, dt=dt: E.tensor_tensor(
                                msb[:, (2 * h + dt) * 512:(2 * h + dt + 1) * 512],
                                msb[:, (2 * h + dt) * 512:(2 * h + dt + 1) * 512],
                                psE[:, 0:512], ALU.add)), deps=[tmm_, t_msb[(h, dt)]])
                            t_msb[(h, dt)] = tad
                            psE_last = tad
                            k2rd[b2] = tmm_
                    tsp = P.dma("pool", vt_d[tt], vstage[:, j * 2048:(j + 1) * 2048],
                                deps=deps_v + [t_msb[(h, dt)] for h in range(H) for dt in range(2)], ch="vsp")
                    t_vspill[tt] = tsp
                vst_rd = tsp
            t_vspF = t_vspill[7]
            t_ms_done = [t_msb[(h, dt)] for h in range(H) for dt in range(2)]
            t_mev = P.dma("pool", ccm_in.ap().rearrange("(g p) v -> p g v", p=128),
                          msb.rearrange("p (g v) -> p g v", g=8), deps=t_ms_done, ch="mo")
            t_agm = P.coll((lambda E: E.collective_compute(
                "AllGather", ALU.bypass, replica_groups=[list(range(NCORE))],
                ins=[ccm_in.ap().opt()], outs=[ccm_out.ap().opt()])), deps=[t_mev])

            qst_rd = [NIL, NIL]
            t_qspill = {tt: [] for tt in range(8)}
            for h in range(H):
                tm1 = proj_full("wqT", 2 * h, psA, "A")
                tm2 = proj_full("wqT", 2 * h + 1, psB, "B")
                ging_h = ging_s[:, h * 1024:(h + 1) * 1024]
                ta = P.op("dve", (lambda E: E.tensor_tensor(s0[:], psA[:], cosq, ALU.mult)), deps=[tm1] + t_rt)
                tb = P.op("dve", (lambda E: E.tensor_tensor(s1[:], psB[:], sinq, ALU.mult)), deps=[tm2] + t_rt)
                tr1 = P.op("dve", (lambda E: E.tensor_tensor(s2[:], s0[:], s1[:], ALU.subtract)), deps=[ta, tb])
                tq1 = P.op("dve", (lambda E, ging_h=ging_h: E.tensor_tensor(
                    qstage[:, 0:1024], s2[:], ging_h, ALU.mult)), deps=[tr1, qst_rd[0]] + t_gi)
                tc_ = P.op("dve", (lambda E: E.tensor_tensor(s0[:], psB[:], cosq, ALU.mult)), deps=[tm2, tr1])
                td = P.op("dve", (lambda E: E.tensor_tensor(s1[:], psA[:], sinq, ALU.mult)), deps=[tm1, tr1])
                tr2 = P.op("dve", (lambda E: E.tensor_tensor(s3[:], s0[:], s1[:], ALU.add)), deps=[tc_, td])
                tq2 = P.op("dve", (lambda E, ging_h=ging_h: E.tensor_tensor(
                    qstage[:, 1024:2048], s3[:], ging_h, ALU.mult)), deps=[tr2, qst_rd[1]] + t_gi)
                ps_free["A"] = td
                ps_free["B"] = tc_
                for half, tqq in ((0, tq1), (1, tq2)):
                    dt = 2 * h + half
                    lsp = NIL
                    for tt in range(8):
                        tsp = P.dma("pool", qs_d[tt, :, dt * 128:(dt + 1) * 128],
                                    qstage[:, half * 1024 + tt * 128:half * 1024 + (tt + 1) * 128],
                                    deps=[tqq], ch=f"qsp{half}")
                        t_qspill[tt].append(tsp)
                        lsp = tsp
                    qst_rd[half] = lsp

            t_qspF = qst_rd[1]
            t_qspF2 = qst_rd[0]
            t_rope_done = tq2
            t_gw = []
            psg_free = [NIL, NIL]
            for m in range(16):
                wb, twg, b = wload("wgT", m)
                ps = psA if m % 2 == 0 else psB
                pk = m % 2
                tm = None
                for n in range(2):
                    for k in range(KT):
                        deps = [twg] + ([psg_free[pk], ps_free["A"], ps_free["B"]] if (n == 0 and k == 0) else [])
                        tm = P.op("pe", (lambda E, wb=wb, k=k, n=n, ps=ps: E.matmul(
                            ps[:, n * 512:(n + 1) * 512], wb[:, k * 128:(k + 1) * 128],
                            xn[:, k * 1024 + n * 512:k * 1024 + (n + 1) * 512],
                            start=(k == 0), stop=(k == KT - 1))), deps=deps)
                wstate["rd"][b] = tm
                tev = P.op("act", (lambda E, ps=ps, m=m: E.activation(
                    gsilu[:, m * 1024:(m + 1) * 1024], ps[:], AF.Silu)),
                    deps=[tm] + (t_cast_sync if m < 8 else []))
                psg_free[pk] = tev
                t_gw.append(tev)
            t_g_last_mm = tm
            t_xn_done = [t for t in [t_gw[-1]]]
            t_xn_free = t_gw

            t_s0 = P.op("dve", (lambda E: E.memset(ssb[:], 0.0)), deps=t_cast_sync + [t_rope_done])
            tprev = t_s0
            mj_rd = NIL
            for jc in range(NCORE):
                tl = P.dma("act", mjbuf[:].rearrange("p (g v) -> p g v", g=8),
                           ccm_out[jc * 1024:(jc + 1) * 1024, :].rearrange("(g p) v -> p g v", p=128),
                           deps=[t_agm, mj_rd] + t_cast_sync, ch="mj")
                for h in range(H):
                    tprev = P.op("dve", (lambda E, jc=jc, h=h: E.scalar_tensor_tensor(
                        ssb[:, h * 1024:(h + 1) * 1024], mjbuf[:, h * 1024:(h + 1) * 1024],
                        coef_c[:, jc * H + h:jc * H + h + 1], ssb[:, h * 1024:(h + 1) * 1024],
                        ALU.mult, ALU.add)), deps=[tl, tprev] + t_cst)
                mj_rd = tprev
            t_sentry = tprev

            tq_rd = [NIL, NIL]
            tv_rd = [NIL, NIL]
            t_sbf = P.op("dve", (lambda E: E.tensor_copy(sbf[:], ssb[:])),
                         deps=[t_sentry, t_qspF, t_qspF2])
            ps_oT_free = {"oc0": NIL, "oc1": NIL, "T": NIL}
            kf_rd = [t_vspF, t_vspF]
            at_rd = t_vspF
            ob_rd = [t_vspF, t_vspF]
            psE_free = psE_last
            og_w = []
            sq_rd = NIL
            for tt in range(8):
                bq = tt % 2
                tql = P.dma("pool", qtile[bq][:], qs_d[tt],
                            deps=[t_qspF, t_qspF2, t_vspF, tq_rd[bq], wstate["rd"][0], wstate["rd"][1]], ch=f"qt{bq}")
                tvl = P.dma("pool", vtile[bq][:], vt_d[tt],
                            deps=[t_vspF, tv_rd[bq], wstate["rd"][bq]], ch=f"vt{bq}")
                for cc in range(2):
                    c = tt * 2 + cc
                    base = cc * 64
                    for h in range(H):
                        bkf = h % 2
                        tkt = []
                        for dti in range(2):
                            dt = 2 * h + dti
                            ttr = P.op("pe", (lambda E, tt=tt, dt=dt, base=base: E.transpose(
                                psT[:, 0:64],
                                ktok[base:base + 64, tt * 1024 + dt * 128:tt * 1024 + (dt + 1) * 128],
                                idb[base:base + 64, base:base + 64])), deps=[tpsT_free, t_id] + t_ktok_all)
                            tev = P.op("dve", (lambda E, bkf=bkf, dti=dti: E.tensor_copy(
                                kfsc[bkf][:, dti * 64:(dti + 1) * 64], psT[:, 0:64])),
                                deps=[ttr, kf_rd[bkf]])
                            tpsT_free = tev
                            tkt.append(tev)
                        tsc = None
                        for dti in range(2):
                            dt = 2 * h + dti
                            deps = [tkt[dti], tql] + ([psE_free] if dti == 0 else [])
                            tsc = P.op("pe", (lambda E, bkf=bkf, dti=dti, dt=dt, base=base, bq=bq: E.matmul(
                                psE[base:base + 64, 0:64], kfsc[bkf][:, dti * 64:(dti + 1) * 64],
                                qtile[bq][:, dt * 128 + base:dt * 128 + base + 64],
                                start=(dti == 0), stop=(dti == 1),
                                tile_position=(0, base))), deps=deps)
                        kf_rd[bkf] = tsc
                        tat = P.op("dve", (lambda E, base=base: E.tensor_tensor(
                            atsc[base:base + 64, 0:64], psE[base:base + 64, 0:64],
                            binm_s[base:base + 64, :], ALU.mult)), deps=[tsc, t_bm, at_rd])
                        psE_free = tat
                        okey = "oc%d" % (c % 2)
                        oc0 = (c % 2) * 512
                        to1 = P.op("pe", (lambda E, base=base, oc0=oc0, h=h, bq=bq: E.matmul(
                            psC[0:64, oc0:oc0 + 512], atsc[base:base + 64, 0:64],
                            vtile[bq][base:base + 64, h * 512:(h + 1) * 512],
                            start=True, stop=False,
                            tile_position=(base, 0))), deps=[tat, tvl, ps_oT_free[okey]])
                        to2 = None
                        for dti in range(2):
                            dt = 2 * h + dti
                            to2 = P.op("pe", (lambda E, dt=dt, base=base, oc0=oc0, h=h, dti=dti, bq=bq: E.matmul(
                                psC[0:64, oc0:oc0 + 512], qtile[bq][:, dt * 128 + base:dt * 128 + base + 64],
                                sbf[:, (2 * h + dti) * 512:(2 * h + dti + 1) * 512],
                                start=False, stop=(dti == 1))), deps=[to1, t_sbf])
                        at_rd = to1
                        tT = []
                        for dti in range(2):
                            dt = 2 * h + dti
                            tt_ = P.op("pe", (lambda E, tt=tt, dt=dt, base=base, dti=dti, bq=bq, h=h: E.matmul(
                                psB[:, dti * 512:(dti + 1) * 512],
                                ktok[base:base + 64, tt * 1024 + dt * 128:tt * 1024 + (dt + 1) * 128],
                                vtile[bq][base:base + 64, h * 512:(h + 1) * 512],
                                start=True, stop=True,
                                tile_position=(base, 0))), deps=[tvl, ps_oT_free["T"]] + t_ktok_all)
                            tT.append(tt_)
                        tu1 = P.op("dve", (lambda E, h=h: E.tensor_tensor(
                            ssb[:, h * 1024:(h + 1) * 1024], ssb[:, h * 1024:(h + 1) * 1024],
                            psB[:], ALU.add)), deps=tT + [t_sentry])
                        g64 = float(GAM[h] ** 64)
                        tu2 = P.op("dve", (lambda E, h=h, g64=g64: E.tensor_scalar(
                            ssb[:, h * 1024:(h + 1) * 1024], ssb[:, h * 1024:(h + 1) * 1024],
                            g64, None, ALU.mult)), deps=[tu1])
                        ps_oT_free["T"] = tu1
                        tsq_ = P.op("act", (lambda E, oc0=oc0: E.activation(
                            sq0[0:64, 0:512], psC[0:64, oc0:oc0 + 512], AF.Square,
                            accum_out=smb[0:64, 0:1])), deps=[to2, sq_rd])
                        tvar = P.op("dve", (lambda E: E.tensor_scalar(
                            smb[0:64, 1:2], smb[0:64, 0:1], 1.0 / 512.0, 1e-5, ALU.mult, ALU.add)), deps=[tsq_])
                        tsqr = P.op("act", (lambda E: E.activation(
                            smb[0:64, 1:2], smb[0:64, 1:2], AF.Sqrt)), deps=[tvar])
                        trec = P.op("dve", (lambda E: E.reciprocal(smb[0:64, 2:3], smb[0:64, 1:2])), deps=[tsqr])
                        bo = c % 2
                        tob = P.op("act", (lambda E, oc0=oc0, bo=bo: E.activation(
                            obsc[bo][0:64, :], psC[0:64, oc0:oc0 + 512], AF.Copy,
                            scale=smb[0:64, 2:3])), deps=[trec, ob_rd[bo]])
                        sq_rd = tsq_
                        tgn = P.op("dve", (lambda E, bo=bo: E.tensor_tensor(
                            obsc[bo][0:64, :], obsc[bo][0:64, :], gnormw_c[0:64, :], ALU.mult)),
                            deps=[tob] + t_cst)
                        ps_oT_free["oc%d" % (c % 2)] = tob
                        for j in range(4):
                            ttr = P.op("pe", (lambda E, j=j, bo=bo: E.transpose(
                                psT[:, 128:192],
                                obsc[bo][0:64, j * 128:(j + 1) * 128], idb[0:64, 0:64])),
                                deps=[tgn, t_id, tpsT_free])
                            m_og = h * 4 + j
                            tgw = P.op("dve", (lambda E, m_og=m_og, c=c: E.tensor_tensor(
                                og[:, m_og * 1024 + c * 64:m_og * 1024 + (c + 1) * 64],
                                psT[:, 128:192],
                                gsilu[:, m_og * 1024 + c * 64:m_og * 1024 + (c + 1) * 64], ALU.mult)),
                                deps=[ttr] + t_gw + t_xn1)
                            tpsT_free = tgw
                            og_w.append(tgw)
                        ob_rd[bo] = tgw
                    t_sbf = P.op("dve", (lambda E: E.tensor_copy(sbf[:], ssb[:])),
                                 deps=[tu2, t_sbf, to2])
                tq_rd[bq] = to2
                tv_rd[bq] = tt_
            t_chunk_pe = tt_
            wstate["rd"] = [t_chunk_pe, t_chunk_pe]

            xs = xdram_stream(xto, "xo", init_dep=t_sbf)
            t_x1 = []
            psg_free2 = [NIL, NIL]
            for m in range(16):
                wb, two, b = wload("woT", m)
                ps = psA if m % 2 == 0 else psB
                pk = m % 2
                tm = None
                for n in range(2):
                    for k in range(KT):
                        deps = [two] + og_w + ([psg_free2[pk], ps_free["A"], ps_free["B"], psg_free[0], psg_free[1]] if (n == 0 and k == 0) else [])
                        tm = P.op("pe", (lambda E, wb=wb, k=k, n=n, ps=ps: E.matmul(
                            ps[:, n * 512:(n + 1) * 512], wb[:, k * 128:(k + 1) * 128],
                            og[:, k * 1024 + n * 512:k * 1024 + (n + 1) * 512],
                            start=(k == 0), stop=(k == KT - 1))), deps=deps)
                wstate["rd"][b] = tm
                xap, tx, set_reader = xs(m)
                tev = P.op("dve", (lambda E, ps=ps, m=m, xap=xap: E.tensor_tensor(
                    xsb[:, m * 1024:(m + 1) * 1024], xap, ps[:], ALU.add)),
                    deps=[tm, tx, t_sentry, mj_rd, t_sbf] + t_rt)
                set_reader(tev)
                psg_free2[pk] = tev
                t_x1.append(tev)
            return t_x1

        t_x = attn_a(cast1 + cast2 + [t_rwb])

        fwbcB = [ab[:, 54784:55808], ab[:, 52992:54016]]
        fact = ab[:, 16384:30720]
        fgsl = [ab[:, 40960:43008], ab[:, 43008:45056]]
        fusl = [ab[:, 45056:47104], ab[:, 47104:49152]]
        fdsl = [ab[:, 49152:50944], ab[:, 50944:52736]]
        mjtmp = af[:, 17408:18432]
        wrow = [af[0:1, 16384:16896], af[0:1, 16896:17408]]

        def ffn(tag, xsrc, war_all, n_exp=E):
            tw_ln, t_lnbc = layernorm(
                "fln",
                (lambda k: (xsb[:, k * 1024:(k + 1) * 1024], xsrc[k], (lambda tok: None))),
                xn, sq_war=war_all)
            P.phase = "ff"
            twl = tw_ln[-1]
            if not FFN_ROUTER:
                xev0 = {}
                for m in range(KT):
                    for n in range(2):
                        xev0[(m, n)] = tw_ln[m]
                return xev0, twl, [war_all, war_all]
            lgc = sm[:, 0:128]
            wfs = sm[:, 128:256]
            scr = sm[:, 256:384]
            psE_fr = [twl, twl]
            t_wT = []
            psTr_fr = twl
            for tb in range(8):
                pc = (tb % 2) * 16
                tm = None
                for k in range(KT):
                    deps = [t_rwb, tw_ln[k]] + ([psE_fr[tb % 2]] if k == 0 else [])
                    tm = P.op("pe", (lambda E, tb=tb, k=k, pc=pc: E.matmul(
                        psE[:, pc:pc + 16], xn[:, k * 1024 + tb * 128:k * 1024 + (tb + 1) * 128],
                        rwb[:, k * 16:(k + 1) * 16], start=(k == 0), stop=(k == KT - 1))), deps=deps)
                lT = lgc[:, tb * 16:(tb + 1) * 16]
                t0 = scr[:, tb * 16:(tb + 1) * 16]
                ge = wfs[:, tb * 16:(tb + 1) * 16]
                mxa = sm[:, 416 + tb:417 + tb]
                mx4 = sm[:, 384 + tb:385 + tb]
                neg = sm[:, 392 + tb:393 + tb]
                zz = sm[:, 400 + tb:401 + tb]
                rz = sm[:, 408 + tb:409 + tb]
                te = P.op("dve", (lambda E, lT=lT, pc=pc: E.tensor_tensor(lT, psE[:, pc:pc + 16], ebias_c, ALU.add)),
                          deps=[tm, twl] + t_cst)
                psE_fr[tb % 2] = te
                tq = P.op("dve", (lambda E, mxa=mxa, lT=lT: E.tensor_reduce(mxa, lT, AX.X, ALU.max)), deps=[te])
                tg = P.op("dve", (lambda E, t0=t0, lT=lT, mxa=mxa: E.tensor_scalar(t0, lT, mxa, None, ALU.is_ge)), deps=[tq])
                ts_ = P.op("dve", (lambda E, t0=t0, lT=lT: E.scalar_tensor_tensor(t0, t0, -1e5, lT, ALU.mult, ALU.add)), deps=[tg])
                for _ in range(2):
                    tq = P.op("dve", (lambda E, mxa=mxa, t0=t0: E.tensor_reduce(mxa, t0, AX.X, ALU.max)), deps=[ts_])
                    tg = P.op("dve", (lambda E, ge=ge, t0=t0, mxa=mxa: E.tensor_scalar(ge, t0, mxa, None, ALU.is_ge)), deps=[tq])
                    ts_ = P.op("dve", (lambda E, t0=t0, ge=ge: E.scalar_tensor_tensor(t0, ge, -1e5, t0, ALU.mult, ALU.add)), deps=[tg])
                t4 = P.op("dve", (lambda E, mx4=mx4, t0=t0: E.tensor_reduce(mx4, t0, AX.X, ALU.max)), deps=[ts_])
                t5 = P.op("dve", (lambda E, neg=neg, mx4=mx4: E.tensor_scalar(neg, mx4, -1.0, None, ALU.mult)), deps=[t4])
                t6 = P.op("act", (lambda E, t0=t0, lT=lT, neg=neg: E.activation(t0, lT, AF.Exp, bias=neg)), deps=[t5, ts_])
                t7 = P.op("dve", (lambda E, ge=ge, lT=lT, mx4=mx4: E.tensor_scalar(ge, lT, mx4, None, ALU.is_ge)), deps=[t4, ts_])
                t8 = P.op("dve", (lambda E, t0=t0, ge=ge: E.tensor_tensor(t0, t0, ge, ALU.mult)), deps=[t6, t7])
                t9 = P.op("dve", (lambda E, zz=zz, t0=t0: E.tensor_reduce(zz, t0, AX.X, ALU.add)), deps=[t8])
                t10 = P.op("dve", (lambda E, rz=rz, zz=zz: E.reciprocal(rz, zz)), deps=[t9])
                t11 = P.op("dve", (lambda E, ge=ge, t0=t0, rz=rz: E.tensor_scalar(ge, t0, rz, None, ALU.mult)), deps=[t10])
                t12 = P.op("pe", (lambda E, ge=ge: E.matmul(
                    psC[0:16, 0:128], ge, idf[:], start=True, stop=True)),
                    deps=[t11, t_idf, psTr_fr])
                t13 = P.op("dve", (lambda E, tb=tb: E.tensor_copy(sm2[0:16, tb * 128:(tb + 1) * 128], psC[0:16, 0:128])),
                           deps=[t12, twl] + t_lnbc)
                psTr_fr = t13
                t_wT.append(t13)

            def mk_loader(bufs, eng, chtag, ncol, slab_fn, t_ag):
                st = {"i": 0, "rd": [war_all, war_all]}

                def load(e, m, extra=()):
                    b = st["i"] % 2
                    st["i"] += 1
                    tl = P.dma(eng, bufs[b][:, 0:ncol], slab_fn(e, m),
                               deps=[st["rd"][b], t_ag] + list(extra), ch=f"{chtag}{b}")
                    return bufs[b], tl, b
                return load, st
            gload, gst = mk_loader(fgsl, "sp", "fg", 2048, slab2g, t_ag2a)
            uload, ust = mk_loader(fusl, "pool", "fu", 2048, slab2u, t_ag2a)
            dload, dst_ = mk_loader(fdsl, "act", "fd", 1792, slab2d, t_ag2b)

            S = {"psE": [psE_fr[0], psE_fr[1], psTr_fr]}
            row_rd = [twl, twl]
            fw_rd = [war_all, war_all]

            def fwbc_build(e):
                eb2 = e % 2
                toks = []
                for n2 in range(2):
                    tx1 = P.op("pe", (lambda E, e=e, n2=n2: E.matmul(
                        psE[0:1, 0:512], idf[0:16, e:e + 1], sm2[0:16, n2 * 512:(n2 + 1) * 512],
                        start=True, stop=True)), deps=S["psE"] + t_wT + [t_idf])
                    tx2 = P.op("dve", (lambda E, n2=n2: E.tensor_copy(wrow[n2], psE[0:1, 0:512])),
                               deps=[tx1, row_rd[n2]])
                    tx3 = P.op("pe", (lambda E, n2=n2: E.matmul(
                        psE[:, 0:512], onesrf[:], wrow[n2], start=True, stop=True)),
                        deps=[tx2, t_onesr])
                    row_rd[n2] = tx3
                    tx4 = P.op("act", (lambda E, eb2=eb2, n2=n2: E.activation(
                        fwbcB[eb2][:, n2 * 512:(n2 + 1) * 512], psE[:, 0:512], AF.Copy)),
                        deps=[tx3, fw_rd[eb2]])
                    S["psE"] = [tx4]
                    toks.append(tx4)
                return toks

            psA_fr = [twl, twl]
            psB_fr = [twl, twl]
            psC_fr = [twl, twl]
            mjtmp_rd = [twl, twl]
            fact_rd = {(i, n): t_ag2b for i in range(IM) for n in range(2)}
            prev_down = [twl, twl]
            xev = {}
            for m in range(KT):
                for n in range(2):
                    xev[(m, n)] = tw_ln[m]
            pool_last = [war_all, war_all]
            tdm = twl
            for e in range(n_exp):
                wb_toks = fwbc_build(e)
                pool_tok = {}
                for i in range(IM):
                    gb, tgl, bg = gload(e, i)
                    ub, tul, bu = uload(e, i)
                    tmg_last = tmu_last = None
                    for n in range(2):
                        tmg = None
                        for k in range(KT):
                            deps = [tgl] + ([psA_fr[n]] if k == 0 else []) \
                                + (tw_ln if (e == 0 and i == 0 and k == 0) else [])
                            tmg = P.op("pe", (lambda E, gb=gb, k=k, n=n: E.matmul(
                                psA[:, n * 512:(n + 1) * 512], gb[:, k * 128:(k + 1) * 128],
                                xn[:, k * 1024 + n * 512:k * 1024 + (n + 1) * 512],
                                start=(k == 0), stop=(k == KT - 1))), deps=deps)
                        tmu = None
                        for k in range(KT):
                            deps = [tul] + ([psB_fr[n]] if k == 0 else [])
                            tmu = P.op("pe", (lambda E, ub=ub, k=k, n=n: E.matmul(
                                psB[:, n * 512:(n + 1) * 512], ub[:, k * 128:(k + 1) * 128],
                                xn[:, k * 1024 + n * 512:k * 1024 + (n + 1) * 512],
                                start=(k == 0), stop=(k == KT - 1))), deps=deps)
                        tsil = P.op("act", (lambda E, n=n: E.activation(
                            mjtmp[:, n * 512:(n + 1) * 512], psA[:, n * 512:(n + 1) * 512], AF.Silu)),
                            deps=[tmg, mjtmp_rd[n]])
                        psA_fr[n] = tsil
                        tmul = P.op("dve", (lambda E, i=i, n=n: E.tensor_tensor(
                            fact[:, i * 1024 + n * 512:i * 1024 + (n + 1) * 512],
                            mjtmp[:, n * 512:(n + 1) * 512], psB[:, n * 512:(n + 1) * 512], ALU.mult)),
                            deps=[tsil, tmu, fact_rd[(i, n)], prev_down[n]])
                        psB_fr[n] = tmul
                        mjtmp_rd[n] = tmul
                        tscale = P.op("pool", (lambda E, i=i, n=n, eb2=e % 2: E.tensor_tensor(
                            fact[:, i * 1024 + n * 512:i * 1024 + (n + 1) * 512],
                            fact[:, i * 1024 + n * 512:i * 1024 + (n + 1) * 512],
                            fwbcB[eb2][:, n * 512:(n + 1) * 512], ALU.mult)),
                            deps=[tmul] + wb_toks)
                        pool_tok[(i, n)] = tscale
                        fact_rd[(i, n)] = tscale
                        tmg_last, tmu_last = tmg, tmu
                    gst["rd"][bg] = tmg_last
                    ust["rd"][bu] = tmu_last
                pool_last[e % 2] = pool_tok[(IM - 1, 1)]
                fw_rd[e % 2] = pool_tok[(IM - 1, 1)]
                for m in range(KT):
                    db, tdl, bd = dload(e, m)
                    for n in range(2):
                        tdm = None
                        for i in range(IM):
                            deps = [tdl, pool_tok[(i, n)]] + ([psC_fr[n]] if i == 0 else [])
                            tdm = P.op("pe", (lambda E, db=db, i=i, n=n: E.matmul(
                                psC[:, n * 512:(n + 1) * 512], db[:, i * 128:(i + 1) * 128],
                                fact[:, i * 1024 + n * 512:i * 1024 + (n + 1) * 512],
                                start=(i == 0), stop=(i == IM - 1))), deps=deps)
                        tev = P.op("dve", (lambda E, m=m, n=n: E.tensor_tensor(
                            xsb[:, m * 1024 + n * 512:m * 1024 + (n + 1) * 512],
                            xsb[:, m * 1024 + n * 512:m * 1024 + (n + 1) * 512],
                            psC[:, n * 512:(n + 1) * 512], ALU.add)),
                            deps=[tdm, xev[(m, n)]])
                        psC_fr[n] = tev
                        xev[(m, n)] = tev
                    dst_["rd"][bd] = tdm
                prev_down = [tdm, tdm]
            return xev, tdm, pool_last

        MULT = 1.0 / float(np.sqrt(128.0))

        def attn_b(xsrc, war_dve, war_pool):
            kb_sb = ab[:, 0:16384]
            vb_sb = ab[:, 16384:32896]
            xnr = ab[:, 32896:49280]
            wqb = [ab[:, 49280:51328], ab[:, 51328:53376]]
            qtb = [ab[:, 53376:54400], ab[:, 54400:55424]]
            pP = [ab[:, 55808:56320], ab[:, 56320:56832]]
            ystage = ab[:, 56832:57344]
            vstg = [ab[:, 57344:58368], ab[:, 58368:59392]]
            tri_s = ab[:, 59392:59520]

            tw3, t_ln3bc = layernorm(
                "bln",
                (lambda k: (xsb[:, k * 1024:(k + 1) * 1024], xsrc[k], (lambda tok: None))),
                xn, sq_war=war_dve)
            P.phase = "ab"
            twl = tw3[-1]
            t_xno = P.dma("pool", ccxn_in.ap().rearrange("(m p) t -> p m t", p=128),
                          xn.rearrange("p (m t) -> p m t", m=16), deps=tw3 + [war_pool], ch="xno")
            t_agx = P.coll((lambda E: E.collective_compute(
                "AllGather", ALU.bypass, replica_groups=[list(range(NCORE))],
                ins=[ccxn_in.ap().opt()], outs=[ccxn_out.ap().opt()])), deps=[t_xno])
            t_tri = P.dma("sp", tri_s[:], tri_i, deps=[war_dve], ch="tri")

            wq_rd = [war_dve, war_dve]
            xnr_rd = [war_dve, war_dve]
            vstg_rd = [war_dve, war_dve]
            psum_fr = {0: twl, 1: twl}
            t_kevac = {}
            t_vb = {}
            t_qspill = {}
            psT_fr = war_dve
            t_ydma = []
            for r in range(NCORE):
                t_xl = []
                for half in range(2):
                    tl = P.dma("sp", xnr[:, half * 8192:(half + 1) * 8192].rearrange("p (k t) -> p k t", k=8),
                               ccxn_out[r * D + half * 1024:r * D + (half + 1) * 1024, :]
                               .rearrange("(k p) t -> p k t", p=128),
                               deps=[t_agx, xnr_rd[half]], ch=f"xn{half}")
                    t_xl.append(tl)
                for m in range(6):
                    b = m % 2
                    twq = P.dma("act", wqb[b][:], wqkv_bf[m * 128:(m + 1) * 128, :],
                                deps=[wq_rd[b]] + cast2, ch=f"wq{b}")
                    ps = psA if m % 2 == 0 else psB
                    pk = m % 2
                    tm = None
                    for n in range(2):
                        for k in range(KT):
                            deps = [twq, t_xl[k // 8]] + ([psum_fr[pk]] if (n == 0 and k == 0) else [])
                            tm = P.op("pe", (lambda E, b=b, k=k, n=n, ps=ps: E.matmul(
                                ps[:, n * 512:(n + 1) * 512], wqb[b][:, k * 128:(k + 1) * 128],
                                xnr[:, k * 1024 + n * 512:k * 1024 + (n + 1) * 512],
                                start=(k == 0), stop=(k == KT - 1))), deps=deps)
                    wq_rd[b] = tm
                    if m >= 4:
                        xnr_rd = [tm, tm]
                    h = m % 2
                    if m < 2:
                        bs = 0
                        tev = P.op("act", (lambda E, ps=ps, h=h: E.activation(
                            vstg[0][:], ps[:], AF.Identity, bias=bqkv_c[:, h:h + 1])),
                            deps=[tm, vstg_rd[0]] + t_cst)
                        psum_fr[pk] = tev
                        tsp = P.dma("pool", qb_d.ap()[r * 8:(r + 1) * 8, :, h * 128:(h + 1) * 128]
                                    .rearrange("j p q -> p j q"),
                                    vstg[0].rearrange("p (j q) -> p j q", j=8),
                                    deps=[tev], ch="qsp")
                        t_qspill[(r, h)] = tsp
                        vstg_rd[0] = tsp
                    elif m < 4:
                        tev = P.op("act", (lambda E, ps=ps, h=h, r=r, m=m: E.activation(
                            kb_sb[:, h * 8192 + r * 1024:h * 8192 + (r + 1) * 1024], ps[:],
                            AF.Identity, bias=bqkv_c[:, m:m + 1])),
                            deps=[tm, t_agx] + t_cst)
                        psum_fr[pk] = tev
                        t_kevac[(h, r)] = tev
                    else:
                        tev = P.op("act", (lambda E, ps=ps, m=m: E.activation(
                            vstg[1][:], ps[:], AF.Identity, bias=bqkv_c[:, m:m + 1])),
                            deps=[tm, vstg_rd[1]] + t_cst)
                        psum_fr[pk] = tev
                        for j in range(8):
                            kbg = r * 8 + j
                            ttr = P.op("pe", (lambda E, j=j: E.transpose(
                                psT[:, 0:128], vstg[1][:, j * 128:(j + 1) * 128], idb[:])),
                                deps=[tev, t_id, psT_fr])
                            tve = P.op("dve", (lambda E, h=h, kbg=kbg: E.tensor_copy(
                                vb_sb[:, (h * 64 + kbg) * 129:(h * 64 + kbg) * 129 + 128],
                                psT[:, 0:128])), deps=[ttr, war_dve])
                            psT_fr = tve
                            tme = P.op("pool", (lambda E, h=h, kbg=kbg: E.memset(
                                vb_sb[:, (h * 64 + kbg) * 129 + 128:(h * 64 + kbg) * 129 + 129], 1.0)),
                                deps=[tve, war_pool])
                            t_vb[(h, kbg)] = tme
                        vstg_rd[1] = ttr

            psA_fr2 = [t_qspill[(7, 1)], t_qspill[(7, 1)]]
            qtb_rd = [war_dve, war_dve]
            pnum = [psC, psC, psB, psB]
            SUBC = [0, 512, 0, 512]
            pnum_fr = {s: psum_fr[s % 2] for s in range(4)}
            ytr_rd = t_qspill[(7, 0)]
            yst_rd = [twl] * 4
            smb_rd = twl
            tri_rd = t_tri
            for qB in range(16):
                bq = qB % 2
                r0 = qB // 2
                tql = P.dma("pool", qtb[bq].rearrange("p (h j q) -> p h j q", h=2, j=4),
                            qb_d.ap()[qB * 4:(qB + 1) * 4, :, :].rearrange("j p (h q) -> p h j q", h=2),
                            deps=[t_qspill[(r0, 0)], t_qspill[(r0, 1)], qtb_rd[bq]], ch=f"qt{bq}")
                for h in range(2):
                    texp = None
                    for kb in range(4 * qB + 4):
                        b = kb % 2
                        rk = kb // 8
                        tsc = P.op("pe", (lambda E, b=b, h=h, kb=kb, bq=bq: E.matmul(
                            psA[:, b * 512:(b + 1) * 512],
                            kb_sb[:, h * 8192 + kb * 128:h * 8192 + (kb + 1) * 128],
                            qtb[bq][:, h * 512:(h + 1) * 512],
                            start=True, stop=True)),
                            deps=[tql, t_kevac[(h, rk)], psA_fr2[b]])
                        j = kb - 4 * qB
                        if j >= 0:
                            toks = []
                            if j > 0:
                                tz = P.op("pool", (lambda E, b=b, j=j: E.memset(
                                    pP[b][:, 0:j * 128], 0.0)), deps=[tsc, war_pool])
                                toks.append(tz)
                            tdg = P.op("dve", (lambda E, b=b, j=j: E.tensor_tensor(
                                psA[:, b * 512 + j * 128:b * 512 + (j + 1) * 128],
                                psA[:, b * 512 + j * 128:b * 512 + (j + 1) * 128],
                                tri_s[:], ALU.add)), deps=[tsc, tri_rd])
                            texp = P.op("act", (lambda E, b=b, j=j: E.activation(
                                pP[b][:, j * 128:512], psA[:, b * 512 + j * 128:b * 512 + 512],
                                AF.Exp, scale=MULT)), deps=[tdg] + toks)
                            tri_rd = texp
                        else:
                            texp = P.op("act", (lambda E, b=b: E.activation(
                                pP[b][:], psA[:, b * 512:(b + 1) * 512], AF.Exp, scale=MULT)),
                                deps=[tsc])
                        psA_fr2[b] = texp
                        for s in range(4):
                            tnm = P.op("pe", (lambda E, b=b, h=h, kb=kb, s=s, qB=qB: E.matmul(
                                pnum[s][:, SUBC[s]:SUBC[s] + 129],
                                pP[b][:, s * 128:(s + 1) * 128],
                                vb_sb[:, (h * 64 + kb) * 129:(h * 64 + kb + 1) * 129],
                                start=(kb == 0), stop=(kb == 4 * qB + 3))),
                                deps=[texp, t_vb[(h, kb)]] + ([pnum_fr[s]] if kb == 0 else []))
                            if kb == 4 * qB + 3:
                                pnum_fr[s] = tnm
                    tyes = []
                    for s in range(4):
                        rc = smb[:, h * 4 + s:h * 4 + s + 1]
                        trc = P.op("dve", (lambda E, h=h, s=s, rc=rc: E.reciprocal(
                            rc, pnum[s][:, SUBC[s] + 128:SUBC[s] + 129])),
                            deps=[pnum_fr[s], smb_rd])
                        tya = P.op("act", (lambda E, h=h, s=s, rc=rc: E.activation(
                            ystage[:, s * 128:(s + 1) * 128], pnum[s][:, SUBC[s]:SUBC[s] + 128],
                            AF.Copy, scale=rc)), deps=[trc, yst_rd[s]])
                        pnum_fr[s] = tya
                        ttr = P.op("pe", (lambda E, s=s: E.transpose(
                            psT[:, 0:128], ystage[:, s * 128:(s + 1) * 128], idb[:])),
                            deps=[tya, t_id, psT_fr])
                        tye = P.op("dve", (lambda E, h=h, s=s: E.tensor_copy(
                            vstg[0][:, h * 512 + s * 128:h * 512 + (s + 1) * 128], psT[:, 0:128])),
                            deps=[ttr, ytr_rd])
                        psT_fr = tye
                        yst_rd[s] = ttr
                        tyes.append(tye)
                    smb_rd = tyes[0]
                    tyd = P.dma("act", ccy_in[h * 128:(h + 1) * 128, qB * 512:(qB + 1) * 512],
                                vstg[0][:, h * 512:(h + 1) * 512], deps=tyes, ch="yo")
                    ytr_rd = tyd
                    t_ydma.append(tyd)
                qtb_rd[bq] = tnm
            t_agy = P.coll((lambda E: E.collective_compute(
                "AllGather", ALU.bypass, replica_groups=[list(range(NCORE))],
                ins=[ccy_in.ap().opt()], outs=[ccy_out.ap().opt()])), deps=t_ydma)

            P.phase = "abo"
            ybuf = ab[:, 0:16384]
            t_yl = {}
            CIDV = {}
            for k in range(KT):
                def mk_yload(k=k):
                    def fn(E):
                        if "v" not in CIDV:
                            CIDV["v"] = nc.values_load(
                                cid_sb[0:1, 0:1], engines=[mybir.EngineType.Pool],
                                min_val=0, max_val=7, skip_runtime_bounds_check=True)
                        return E.dma_start(ybuf[:, k * 1024:(k + 1) * 1024],
                                           ccy_out[k * 128:(k + 1) * 128, bass.ts(CIDV["v"], TC)])
                    return fn
                tl = P.op("pool", mk_yload(), deps=[t_agy, t_cid, tnm], dma=True, ch=f"yl{k % 2}")
                t_yl[k] = tl
            t_ylF = [t_yl[KT - 2], t_yl[KT - 1]]
            xevb = []
            psum_fr2 = [pnum_fr[3], pnum_fr[3]]
            last = None
            for m in range(KT):
                b = m % 2
                twp = P.dma("act", wqb[b][:], slab1("wprojT", m),
                            deps=[wq_rd[b], t_ag1], ch=f"wq{b}")
                ps = psA if m % 2 == 0 else psB
                pk = m % 2
                tm = None
                for n in range(2):
                    for k in range(KT):
                        deps = [twp, t_ylF[k % 2]] + ([psum_fr2[pk], psA_fr2[0], psA_fr2[1]] if (n == 0 and k == 0) else [])
                        tm = P.op("pe", (lambda E, b=b, k=k, n=n, ps=ps: E.matmul(
                            ps[:, n * 512:(n + 1) * 512], wqb[b][:, k * 128:(k + 1) * 128],
                            ybuf[:, k * 1024 + n * 512:k * 1024 + (n + 1) * 512],
                            start=(k == 0), stop=(k == KT - 1))), deps=deps)
                wq_rd[b] = tm
                tev = P.op("dve", (lambda E, ps=ps, m=m: E.scalar_tensor_tensor(
                    xsb[:, m * 1024:(m + 1) * 1024], ps[:], bproj_c[:, m:m + 1],
                    xsb[:, m * 1024:(m + 1) * 1024], ALU.add, ALU.add)),
                    deps=[tm, xsrc[m]] + t_cst)
                psum_fr2[pk] = tev
                xevb.append(tev)
                last = tev
            return xevb, last

        if n_sub >= 2:
            xev1, t_f1_pe, t_f1_pool = ffn("f1", t_x, t_x[-1], n_exp=FFN_EXPERTS)
            xw = [xev1[(m, 1)] for m in range(KT)] + [xev1[(m, 0)] for m in range(KT)]
        else:
            xw = t_x
        if n_sub >= 3:
            xsrc3 = [xev1[(m, 1)] for m in range(KT)]
            xevb, t_b_last = attn_b(xsrc3, xev1[(KT - 1, 1)], t_f1_pool[0])
            xw = xevb
        if n_sub >= 4:
            xev2, t_f2_pe, t_f2_pool = ffn("f2", xevb, t_b_last, n_exp=FFN_EXPERTS)
            xw = [xev2[(m, 1)] for m in range(KT)] + [xev2[(m, 0)] for m in range(KT)]
        P.phase = "out"
        t_out = P.dma("sp", out_t.rearrange("(m p) t -> p m t", p=128),
                      xsb.rearrange("p (m t) -> p m t", m=16), deps=xw, ch="out")
        P.wait("sp", t_out)

        print(f"[build] ops={P.n_ops} sems={len(P.sems)}")
        P.finalize()
    return nc



def _prep_inputs(inputs):
    g1 = _manifest()
    off1, n1 = _offsets(g1)
    x = np.asarray(inputs["L0_attn_a_x"])[0]
    xf = np.ascontiguousarray(x.T)

    blob1 = np.empty(n1, dtype=np.float32)
    for (name, m, cols), src in zip(
            g1, [np.asarray(inputs["L0_attn_a_Wq"]).T, np.asarray(inputs["L0_attn_a_Wk"]).T,
                 np.asarray(inputs["L0_attn_a_Wv"]).T, np.asarray(inputs["L0_attn_a_Wg"]).T,
                 np.asarray(inputs["L0_attn_a_Wo"]).T, np.asarray(inputs["L0_attn_b_w_proj"])]):
        o, mm_, cols_ = off1[name]
        blob1[o:o + mm_ * 128 * cols_] = _pretile(np.ascontiguousarray(src)).ravel()
    gate = np.asarray(inputs["L0_ffn_gate_proj"])
    up = np.asarray(inputs["L0_ffn_up_proj"])
    down = np.asarray(inputs["L0_ffn_down_proj"])
    blob2a = np.empty((E * 2 * IM * 128, D), dtype=np.float32)
    blob2b = np.empty((E * 16 * 128, FI), dtype=np.float32)
    for e in range(E):
        blob2a[(e * 2 * IM) * 128:(e * 2 * IM + IM) * 128] = _pretile(gate[e]).reshape(IM * 128, D)
        blob2a[(e * 2 * IM + IM) * 128:(e * 2 * IM + 2 * IM) * 128] = _pretile(up[e]).reshape(IM * 128, D)
        blob2b[(e * 16) * 128:(e + 1) * 16 * 128] = _pretile(down[e]).reshape(16 * 128, FI)

    w_qkv = np.asarray(inputs["L0_attn_b_w_qkv"])
    b_qkv = np.asarray(inputs["L0_attn_b_b_qkv"])
    b_proj = np.asarray(inputs["L0_attn_b_b_proj"])
    rw = np.asarray(inputs["L0_ffn_router_w"])
    ebias = np.asarray(inputs["L0_ffn_expert_bias"])
    gnw = np.asarray(inputs["L0_attn_a_gnorm_w"])

    ging, gkinv, gkm, binm = _decay_tables()
    tri = _tri_mask()
    shared = dict(
        ging=ging, gkinv=gkinv, gkm=gkm, binm=binm, tri=tri,
        identb=np.eye(128, dtype=bf), identf=np.eye(128, dtype=np.float32),
        onesf=np.ones((128, 1), np.float32), onesrowf=np.ones((1, 128), np.float32),
        rwT=np.ascontiguousarray(rw.T),
        ebias=np.ascontiguousarray(np.broadcast_to(ebias[None, :], (128, 16))),
        gnormw=np.ascontiguousarray(np.broadcast_to(gnw[None, :], (128, DV))),
        bproj=np.ascontiguousarray(b_proj.reshape(16, 128).T),
    )
    in_maps = []
    sh1 = n1 // NCORE
    s2a = blob2a.shape[0] // NCORE
    s2b = blob2b.shape[0] // NCORE
    for r in range(NCORE):
        cols = []
        for part, base in ((0, 0), (1, D), (2, 2 * D)):
            cols.extend(range(base + 2 * r * 128, base + (2 * r + 2) * 128))
        wslice = np.ascontiguousarray(w_qkv[:, cols])
        bslice = b_qkv[cols]
        m = dict(shared)
        m.update(
            xT=np.ascontiguousarray(xf[:, r * TC:(r + 1) * TC]),
            wsh1=blob1[r * sh1:(r + 1) * sh1].reshape(-1, 1024),
            wsh2a=np.ascontiguousarray(blob2a[r * s2a:(r + 1) * s2a]),
            wsh2b=np.ascontiguousarray(blob2b[r * s2b:(r + 1) * s2b]),
            wqkvT=_pretile(wslice).reshape(6, 128, D),
            bqkv=np.ascontiguousarray(bslice.reshape(6, 128).T),
            ropet=_rope_tables(r),
            coef=_coef_table(r),
            cid=np.array([[r]], np.int32),
        )
        in_maps.append(m)
    return in_maps


_NC_CACHE = {}


def kernel(**inputs):
    in_maps = _prep_inputs(inputs)
    key = N_SUB
    if key not in _NC_CACHE:
        _NC_CACHE[key] = build_nc(N_SUB)
    nc = _NC_CACHE[key]
    res = run_bass_kernel_spmd(nc, in_maps, core_ids=list(range(NCORE)))
    outs = [np.asarray(res.results[r]["out"]) for r in range(NCORE)]
    x = np.concatenate([o.T for o in outs], axis=0)
    return x[None, :, :]


# version 03 (final)
# speedup vs baseline: 1.0126x; 1.0126x over previous
import numpy as np
import ml_dtypes
from contextlib import ExitStack

import concourse.bass as bass
import concourse.mybir as mybir
from concourse.bass_utils import run_bass_kernel_spmd

F32 = mybir.dt.float32
BF16 = mybir.dt.bfloat16
AF = mybir.ActivationFunctionType
ALU = mybir.AluOpType
AX = mybir.AxisListType
bf = ml_dtypes.bfloat16

NCORE = 8
T = 8192
TC = 1024
D = 2048
KT = 16
H = 4
DK = 256
DV = 512
CH = 64
NCH = TC // CH
HB = 16
HD = 128
E = 16
FI = 1792
IM = FI // 128
GAM = [float(1.0 - 2.0 ** (-5 - h)) for h in range(H)]

N_SUB = 4
FFN_EXPERTS = 16
FFN_ROUTER = True



def _pretile(w):
    din, dout = w.shape
    k, m = din // 128, dout // 128
    t = w.reshape(k, 128, m, 128).transpose(2, 1, 0, 3).reshape(m, 128, din)
    return np.ascontiguousarray(t, dtype=np.float32)


def _manifest():
    g1 = [("wqT", 8, D), ("wkT", 8, D), ("wvT", 16, D), ("wgT", 16, D), ("woT", 16, D), ("wprojT", 16, D)]
    return g1


def _offsets(group):
    off, o = {}, 0
    for name, m, cols in group:
        off[name] = (o, m, cols)
        o += m * 128 * cols
    return off, o


def _rope_tables(core):
    i = np.arange(128, dtype=np.float64)
    inv_freq = (1.0 / 10000.0 ** ((2 * i) / 256.0)).astype(np.float32)
    pos = (core * TC + np.arange(TC)).astype(np.float32)
    ang = (pos[None, :] * inv_freq[:, None]).astype(np.float32)
    cos = np.cos(ang.astype(np.float64)).astype(np.float32)
    sin = np.sin(ang.astype(np.float64)).astype(np.float32)
    return np.stack([cos / 16.0, sin / 16.0, cos, sin])


def _decay_tables():
    tmod = np.arange(TC) % CH
    ging = np.stack([np.power(g, tmod + 1.0) for g in GAM]).astype(np.float32)
    ging = np.ascontiguousarray(np.broadcast_to(ging[:, None, :], (H, 128, TC))).astype(bf)
    p = np.arange(128) % CH
    gkinv = np.stack([np.power(g, -(p + 1.0)) for g in GAM], axis=1).astype(np.float32)
    cofp = np.arange(128) // CH
    gkm = np.zeros((128, NCH // 2 * H), dtype=np.float64)
    for tt in range(NCH // 2):
        for h in range(H):
            gkm[:, tt * H + h] = np.power(GAM[h], -64.0 * (2 * tt + cofp))
    gkm = gkm.astype(np.float32)
    j = np.arange(128) % CH
    ii = np.arange(CH)
    binm = (ii[None, :] >= j[:, None]).astype(bf)
    return ging, gkinv, gkm, binm


def _coef_table(core):
    c = np.zeros((NCORE, H), dtype=np.float64)
    for jj in range(core):
        for h in range(H):
            c[jj, h] = GAM[h] ** (1024.0 * (core - jj))
    c = c.reshape(-1).astype(np.float32)
    return np.ascontiguousarray(np.broadcast_to(c[None, :], (128, NCORE * H)))


def _tri_mask():
    i = np.arange(128)
    return np.where(i[:, None] <= i[None, :], np.float32(0.0), np.float32(-1e30)).astype(bf)


ENGS = ("pe", "dve", "act", "pool", "sp")
_BLOCK_ATTR = {"pe": "tensor", "dve": "vector", "act": "scalar", "pool": "gpsimd", "sp": "sync"}


class Prog:
    def __init__(self, nc, ctx):
        self.nc = nc
        self.ctx = ctx
        self.streams = {e: [] for e in ENGS}
        self.sems = {}
        self.cnt = {}
        self.waited = {e: {} for e in ENGS}
        self.phase = "p0"
        self.n_ops = 0

    def _sem(self, name):
        if name not in self.sems:
            self.sems[name] = self.ctx.enter_context(self.nc.semaphore(name))
            self.cnt[name] = 0
        return self.sems[name]

    def wait(self, eng, tok):
        s, v = tok
        if self.waited[eng].get(s, 0) >= v:
            return
        self.waited[eng][s] = v
        self._sem(s)
        self.streams[eng].append(("w", s, v))

    def op(self, eng, fn, deps=(), dma=False, ch=None, sem_name=None):
        byname = {}
        for d in deps:
            if d is None:
                continue
            s, v = d
            if byname.get(s, 0) < v:
                byname[s] = v
        for s, v in byname.items():
            self.wait(eng, (s, v))
        if sem_name is not None:
            sname = sem_name
        elif dma:
            sname = f"D{eng}_{self.phase}_{ch or 'x'}"
        else:
            sname = f"S{eng}_{self.phase}"
        self._sem(sname)
        step = 16 if dma else 1
        self.cnt[sname] += step
        self.streams[eng].append(("i", fn, sname, step))
        self.n_ops += 1
        return (sname, self.cnt[sname])

    def dma(self, eng, dst, src, deps=(), ch=None, **kw):
        return self.op(eng, (lambda E, dst=dst, src=src, kw=kw: E.dma_start(dst, src, **kw)),
                       deps, dma=True, ch=ch)

    _ncoll = 0

    def coll(self, fn, deps=()):
        Prog._ncoll += 1
        return self.op("pool", fn, deps, sem_name=f"CC{Prog._ncoll}")

    def finalize(self):
        blk = self.ctx.enter_context(self.nc.Block())
        for ename in ENGS:
            stream = self.streams[ename]
            sems = self.sems

            def run(EO, stream=stream, sems=sems):
                for item in stream:
                    if item[0] == "w":
                        EO.wait_ge(sems[item[1]], item[2])
                    else:
                        item[1](EO).then_inc(sems[item[2]], item[3])

            getattr(blk, _BLOCK_ATTR[ename])(run)



def build_nc(n_sub=4):
    nc = bass.Bass("TRN2", target_bir_lowering=False)
    g1 = _manifest()
    off1, n1 = _offsets(g1)
    assert n1 % (NCORE * 1024) == 0
    sh1 = n1 // NCORE
    R2A = E * 2 * IM * 128
    R2B = E * 16 * 128

    xT = nc.dram_tensor("xT", [D, TC], F32, kind="ExternalInput").ap()
    wsh1 = nc.dram_tensor("wsh1", [sh1 // 1024, 1024], F32, kind="ExternalInput").ap()
    wsh2a = nc.dram_tensor("wsh2a", [R2A // NCORE, D], F32, kind="ExternalInput").ap()
    wsh2b = nc.dram_tensor("wsh2b", [R2B // NCORE, FI], F32, kind="ExternalInput").ap()
    wqkvT_i = nc.dram_tensor("wqkvT", [6, 128, D], F32, kind="ExternalInput").ap()
    bqkv_i = nc.dram_tensor("bqkv", [128, 6], F32, kind="ExternalInput").ap()
    bproj_i = nc.dram_tensor("bproj", [128, 16], F32, kind="ExternalInput").ap()
    rwT_i = nc.dram_tensor("rwT", [D, 16], F32, kind="ExternalInput").ap()
    ebias_i = nc.dram_tensor("ebias", [128, 16], F32, kind="ExternalInput").ap()
    gnormw_i = nc.dram_tensor("gnormw", [128, DV], F32, kind="ExternalInput").ap()
    ropet_i = nc.dram_tensor("ropet", [4, 128, TC], F32, kind="ExternalInput").ap()
    ging_i = nc.dram_tensor("ging", [H, 128, TC], BF16, kind="ExternalInput").ap()
    gkinv_i = nc.dram_tensor("gkinv", [128, H], F32, kind="ExternalInput").ap()
    gkm_i = nc.dram_tensor("gkm", [128, NCH // 2 * H], F32, kind="ExternalInput").ap()
    coef_i = nc.dram_tensor("coef", [128, NCORE * H], F32, kind="ExternalInput").ap()
    binm_i = nc.dram_tensor("binm", [128, CH], BF16, kind="ExternalInput").ap()
    tri_i = nc.dram_tensor("tri", [128, 128], BF16, kind="ExternalInput").ap()
    identb_i = nc.dram_tensor("identb", [128, 128], BF16, kind="ExternalInput").ap()
    identf_i = nc.dram_tensor("identf", [128, 128], F32, kind="ExternalInput").ap()
    onesf_i = nc.dram_tensor("onesf", [128, 1], F32, kind="ExternalInput").ap()
    onesrowf_i = nc.dram_tensor("onesrowf", [1, 128], F32, kind="ExternalInput").ap()
    cid_i = nc.dram_tensor("cid", [1, 1], mybir.dt.int32, kind="ExternalInput").ap()
    out_t = nc.dram_tensor("out", [D, TC], F32, kind="ExternalOutput").ap()

    cc1_in = nc.dram_tensor("cc1_in", [sh1 // 1024, 1024], BF16)
    cc1_out = nc.dram_tensor("cc1_out", [n1 // 1024, 1024], BF16)
    cc2a_in = nc.dram_tensor("cc2a_in", [R2A // NCORE, D], BF16)
    cc2a_out = nc.dram_tensor("cc2a_out", [R2A, D], BF16)
    cc2b_in = nc.dram_tensor("cc2b_in", [R2B // NCORE, FI], BF16)
    cc2b_out = nc.dram_tensor("cc2b_out", [R2B, FI], BF16)
    ccm_in = nc.dram_tensor("ccm_in", [1024, DV], F32)
    ccm_out = nc.dram_tensor("ccm_out", [NCORE * 1024, DV], F32)
    ccxn_in = nc.dram_tensor("ccxn_in", [D, TC], BF16)
    ccxn_out = nc.dram_tensor("ccxn_out", [NCORE * D, TC], BF16)
    ccy_in = nc.dram_tensor("ccy_in", [2 * HD, T], BF16)
    ccy_out = nc.dram_tensor("ccy_out", [D, T], BF16)
    qs_d = nc.dram_tensor("qs_d", [NCH // 2, 128, 1024], BF16)
    vt_d = nc.dram_tensor("vt_d", [NCH // 2, 128, 2048], BF16)
    qb_d = nc.dram_tensor("qb_d", [64, 128, 256], BF16)
    wqkv_bf = nc.dram_tensor("wqkv_bf", [6 * 128, D], BF16)

    def slab1(name, m):
        o, nm, cols = off1[name]
        oo = o + m * 128 * cols
        return cc1_out[oo // 1024:(oo + 128 * cols) // 1024, :].rearrange("(p r) c -> p (r c)", p=128)

    def slab2g(e, m):
        r0 = (e * 2 * IM + m) * 128
        return cc2a_out[r0:r0 + 128, :]

    def slab2u(e, m):
        r0 = (e * 2 * IM + IM + m) * 128
        return cc2a_out[r0:r0 + 128, :]

    def slab2d(e, m):
        r0 = (e * 16 + m) * 128
        return cc2b_out[r0:r0 + 128, :]

    with ExitStack() as ctx:
        en = ctx.enter_context
        P = Prog(nc, ctx)

        af = en(nc.sbuf_tensor("s_af", [128, 18432], F32))
        ab = en(nc.sbuf_tensor("s_ab", [128, 62464], BF16))
        sm = en(nc.sbuf_tensor("s_sm", [128, 1024], F32))
        sm2 = en(nc.sbuf_tensor("s_sm2", [128, 1024], F32))
        smb = en(nc.sbuf_tensor("s_smb", [128, 256], F32))
        cst = en(nc.sbuf_tensor("s_cst", [128, 896], F32))
        idb = en(nc.sbuf_tensor("s_idb", [128, 128], BF16))
        cid_sb = en(nc.sbuf_tensor("s_cid", [1, 1], mybir.dt.int32))
        idf = en(nc.sbuf_tensor("s_idf", [128, 128], F32))
        onesf = en(nc.sbuf_tensor("s_onesf", [128, 1], F32))
        onesrf = en(nc.sbuf_tensor("s_onesrf", [1, 128], F32))

        xsb = af[:, 0:16384]
        ropetab = af[:, 0:4096]
        mjbuf = af[:, 4096:8192]
        ssb = af[:, 8192:12288]
        xtl = [af[:, 0:1024], af[:, 2048:3072]]
        xto = [af[:, 16384:17408], af[:, 17408:18432]]
        sq0 = af[:, 16384:17408]
        sq1 = af[:, 17408:18432]
        oacc = [sm2[:, 0:129], sm2[:, 256:385]]

        xn = ab[:, 0:16384]
        og = ab[:, 0:16384]
        gsilu = ab[:, 16384:32768]
        ktok = ab[:, 32768:40960]
        qstage = ab[:, 40960:43008]
        kstage = ab[:, 43008:45056]
        vstage = ab[:, 45056:53248]
        ging_s = ab[:, 53248:57344]
        wsl = [ab[:, 57344:59392], ab[:, 59392:61440]]
        k2b = [ab[:, 61440:61696], ab[:, 61696:61952]]
        sbf = ab[:, 40960:45056]
        kfsc = [ab[:, 45056:45184], ab[:, 45184:45312]]
        atsc = ab[:, 45312:45440]
        obsc = [ab[:, 45440:45952], ab[:, 45952:46464]]
        binm_s = ab[:, 61952:62016]
        qtile = [ab[:, 46592:47616], ab[:, 47616:48640]]
        vtile = [ab[:, 48640:50688], ab[:, 50688:52736]]
        fact = ab[:, 16384:30720]
        fgsl = [ab[:, 40960:43008], ab[:, 43008:45056]]
        fusl = [ab[:, 45056:47104], ab[:, 47104:49152]]
        fdsl = [ab[:, 49152:50944], ab[:, 50944:52736]]
        fgate = ab[:, 52736:53760]
        fub = ab[:, 53760:54784]
        fwbc = ab[:, 54784:55808]
        kb_sb = ab[:, 0:16384]
        vb_sb = ab[:, 16384:32896]
        wqkv_sb = ab[:, 32896:45184]
        xnb = [ab[:, 45184:49280], ab[:, 49280:53376]]
        pP = [ab[:, 53376:54400], ab[:, 54400:55424]]
        pTe = [ab[:, 55424:56448], ab[:, 56448:57472]]
        qbt = [ab[:, 57472:57728], ab[:, 57728:57984]]
        ytmp = ab[:, 57984:58112]
        ystage = ab[:, 58112:59136]
        tri_s = ab[:, 59136:59264]
        vsc_b = ab[:, 59264:59776]

        psA = en(nc.psum_tensor("psA", [128, 1024], F32))
        psB = en(nc.psum_tensor("psB", [128, 1024], F32))
        psC = en(nc.psum_tensor("psC", [128, 1024], F32))
        psT = en(nc.psum_tensor("psT", [128, 1024], BF16))
        psE = en(nc.psum_tensor("psE", [128, 512], F32))

        NIL = ("nil", 0)

        P.phase = "p0"
        P.dma("sp", idb[:], identb_i, ch="cst")
        t_cid = P.dma("sp", cid_sb[:], cid_i, ch="cid")
        P.dma("sp", idf[:], identf_i, ch="cst")
        P.dma("sp", onesf[:], onesf_i, ch="cst")
        P.dma("sp", onesrf[:], onesrowf_i, ch="cst")
        t_cst = []
        gkinv_c = cst[:, 0:H]
        gkm_c = cst[:, 16:16 + NCH // 2 * H]
        coef_c = cst[:, 64:64 + NCORE * H]
        gnormw_c = cst[:, 128:128 + DV]
        ebias_c = cst[:, 768:768 + 16]
        bqkv_c = cst[:, 800:800 + 6]
        bproj_c = cst[:, 832:832 + 16]
        bqs_c = cst[:, 864:864 + 2]
        t_cst.append(P.dma("sp", gkinv_c, gkinv_i, ch="cst"))
        t_cst.append(P.dma("sp", gkm_c, gkm_i, ch="cst"))
        t_cst.append(P.dma("sp", coef_c, coef_i, ch="cst"))
        t_cst.append(P.dma("sp", gnormw_c, gnormw_i, ch="cst"))
        t_cst.append(P.dma("sp", ebias_c, ebias_i, ch="cst"))
        t_cst.append(P.dma("sp", bqkv_c, bqkv_i, ch="cst"))
        t_cst.append(P.dma("sp", bproj_c, bproj_i, ch="cst"))
        t_cstF = t_cst[-1]
        t_id = t_idf = t_ones = t_onesr = t_cstF
        t_cst = [t_cstF]

        def cast_blob(tag, wsh, cc_in, rows):
            f32st = [af[:, 4096:6144], af[:, 6144:8192]]
            bfst = [ab[:, 16384:18432], ab[:, 18432:20480]]
            tprev = [NIL, NIL]
            toks = [NIL, NIL]
            nchunk = rows // 256
            for i in range(nchunk):
                b = i % 2
                src = wsh[i * 256:(i + 1) * 256, :].rearrange("(p r) c -> p (r c)", p=128)
                tl = P.dma("pool", f32st[b][:, 0:2048], src, deps=[tprev[b]], ch=f"{tag}l{b}")
                tc1 = P.op("pool", (lambda E, b=b: E.tensor_copy(bfst[b][:, 0:768], f32st[b][:, 0:768])), deps=[tl])
                tc2 = P.op("pool", (lambda E, b=b: E.tensor_copy(bfst[b][:, 768:1536], f32st[b][:, 768:1536])), deps=[tl])
                tc3 = P.op("pool", (lambda E, b=b: E.tensor_copy(bfst[b][:, 1536:2048], f32st[b][:, 1536:2048])), deps=[tl])
                dst = cc_in[i * 256:(i + 1) * 256, :].rearrange("(p r) c -> p (r c)", p=128)
                ts_ = P.dma("pool", dst, bfst[b][:, 0:2048], deps=[tc1, tc2, tc3], ch=f"{tag}s{b}")
                tprev[b] = ts_
                toks[b] = ts_
            return [t for t in toks if t is not NIL]

        rwb = ab[:, 62016:62272]
        t_rw1 = P.dma("sp", af[:, 8192:8448].rearrange("p (k e) -> p k e", k=16),
                      rwT_i.rearrange("(k p) e -> p k e", p=128), ch="rw")
        t_rwb = P.op("dve", (lambda E: E.tensor_copy(rwb[:], af[:, 8192:8448])), deps=[t_rw1])

        cast1 = cast_blob("c1", wsh1, cc1_in, sh1 // 1024)
        t_ag1 = P.coll((lambda E: E.collective_compute(
            "AllGather", ALU.bypass, replica_groups=[list(range(NCORE))],
            ins=[cc1_in.ap().opt()], outs=[cc1_out.ap().opt()])), deps=cast1)

        def cast_wide(tag, wsh, cc_in, nrows, width, dep0=()):
            f32st = [af[:, 4096:6144], af[:, 6144:8192]]
            bfst = [ab[:, 16384:18432], ab[:, 18432:20480]]
            tprev = [list(dep0) or [NIL], list(dep0) or [NIL]]
            toks = [NIL, NIL]
            c1 = (width // 3) // 2 * 2
            c2 = 2 * c1
            for j in range(nrows // 128):
                b = j % 2
                tl = P.dma("pool", f32st[b][:, 0:width], wsh[j * 128:(j + 1) * 128, :],
                           deps=tprev[b], ch=f"{tag}l{b}")
                tc1 = P.op("pool", (lambda E, b=b, c1=c1: E.tensor_copy(bfst[b][:, 0:c1], f32st[b][:, 0:c1])), deps=[tl])
                tc2 = P.op("pool", (lambda E, b=b, c1=c1, c2=c2: E.tensor_copy(bfst[b][:, c1:c2], f32st[b][:, c1:c2])), deps=[tl])
                tc3 = P.op("pool", (lambda E, b=b, c2=c2, width=width: E.tensor_copy(bfst[b][:, c2:width], f32st[b][:, c2:width])), deps=[tl])
                ts_ = P.dma("pool", cc_in[j * 128:(j + 1) * 128, :], bfst[b][:, 0:width],
                            deps=[tc1, tc2, tc3], ch=f"{tag}s{b}")
                tprev[b] = [ts_]
                toks[b] = ts_
            return [t for t in toks if t is not NIL]

        cast2a = cast_wide("c2a", wsh2a, cc2a_in, R2A // NCORE, D, dep0=cast1)
        t_ag2a = P.coll((lambda E: E.collective_compute(
            "AllGather", ALU.bypass, replica_groups=[list(range(NCORE))],
            ins=[cc2a_in.ap().opt()], outs=[cc2a_out.ap().opt()])), deps=cast2a)
        cast2b = cast_wide("c2b", wsh2b, cc2b_in, R2B // NCORE, FI, dep0=cast2a)
        t_ag2b = P.coll((lambda E: E.collective_compute(
            "AllGather", ALU.bypass, replica_groups=[list(range(NCORE))],
            ins=[cc2b_in.ap().opt()], outs=[cc2b_out.ap().opt()])), deps=cast2b)
        castq = cast_wide("cq", wqkvT_i.rearrange("s p c -> (s p) c"), wqkv_bf, 6 * 128, D,
                          dep0=cast2b)
        cast2 = cast2a + cast2b + castq

        def layernorm(phase, src_tile_fn, out_buf, n_tiles=KT, sq_war=None):
            P.phase = phase
            sqb = [sq0, sq1]
            tsq = [sq_war or NIL, sq_war or NIL]
            tmm = []
            for k in range(n_tiles):
                xap, tx, set_reader = src_tile_fn(k)
                b = k % 2
                tq = P.op("act", (lambda E, xap=xap, b=b: E.activation(sqb[b][:], xap, AF.Square)),
                          deps=[tx, tsq[b]])
                tm2 = None
                for n in range(2):
                    tgt = [(psE[0:1, :], psE[32:33, :]), (psE[64:65, :], psC[0:1, 0:512])][n]
                    tm1 = P.op("pe", (lambda E, xap=xap, n=n, k=k, tgt=tgt: E.matmul(
                        tgt[0], onesf[:], xap[:, n * 512:(n + 1) * 512],
                        start=(k == 0), stop=(k == n_tiles - 1))), deps=[tx, t_ones])
                    tm2 = P.op("pe", (lambda E, b=b, n=n, k=k, tgt=tgt: E.matmul(
                        tgt[1], onesf[:], sqb[b][:, n * 512:(n + 1) * 512],
                        start=(k == 0), stop=(k == n_tiles - 1))), deps=[tq])
                    if k == n_tiles - 1:
                        tmm += [tm1, tm2]
                set_reader(tm2)
                tsq[b] = tm2
            te = P.op("dve", (lambda E: E.memset(smb[0:1, 0:1], 1e-5)))
            tstat = []
            srcs = [(psE[0:1, :], psE[32:33, :]), (psE[64:65, :], psC[0:1, 0:512])]
            for n in range(2):
                sx, sq_ = srcs[n]
                c0 = n * 512
                t1 = P.op("dve", (lambda E, sx=sx, c0=c0: E.tensor_scalar(
                    sm[0:1, c0:c0 + 512], sx, 1.0 / D, None, ALU.mult)), deps=tmm)
                t2 = P.op("dve", (lambda E, sq_=sq_, c0=c0: E.tensor_scalar(
                    sm2[0:1, c0:c0 + 512], sq_, 1.0 / D, None, ALU.mult)), deps=tmm)
                t3 = P.op("dve", (lambda E, c0=c0: E.tensor_tensor(
                    sq0[0:1, c0:c0 + 512], sm[0:1, c0:c0 + 512], sm[0:1, c0:c0 + 512], ALU.mult)),
                    deps=[t1] + [tsq[0], tsq[1]])
                t4 = P.op("dve", (lambda E, c0=c0: E.tensor_tensor(
                    sm2[0:1, c0:c0 + 512], sm2[0:1, c0:c0 + 512], sq0[0:1, c0:c0 + 512], ALU.subtract)), deps=[t2, t3])
                t5 = P.op("act", (lambda E, c0=c0: E.activation(
                    sm2[0:1, c0:c0 + 512], sm2[0:1, c0:c0 + 512], AF.Sqrt, bias=smb[0:1, 0:1])), deps=[t4, te])
                t6 = P.op("dve", (lambda E, c0=c0: E.reciprocal(
                    sq0[0:1, c0:c0 + 512], sm2[0:1, c0:c0 + 512])), deps=[t5, t3])
                t7 = P.op("dve", (lambda E, c0=c0: E.tensor_tensor(
                    sm2[0:1, c0:c0 + 512], sm[0:1, c0:c0 + 512], sq0[0:1, c0:c0 + 512], ALU.mult)), deps=[t6, t5])
                tstat += [t6, t7]
            tbc = []
            for n in range(2):
                tb1 = P.op("pe", (lambda E, n=n: E.matmul(
                    psA[:, n * 512:(n + 1) * 512], onesrf[:], sq0[0:1, n * 512:(n + 1) * 512],
                    start=True, stop=True)), deps=tstat + [t_onesr] + tmm)
                tb2 = P.op("pe", (lambda E, n=n: E.matmul(
                    psB[:, n * 512:(n + 1) * 512], onesrf[:], sm2[0:1, n * 512:(n + 1) * 512],
                    start=True, stop=True)), deps=tstat + [t_onesr])
                tbc += [tb1, tb2]
            tw = []
            tscr = [NIL, NIL]
            for k in range(n_tiles):
                xap, tx, set_reader = src_tile_fn(k)
                b = k % 2
                sc = sqb[b]
                ta = P.op("dve", (lambda E, xap=xap, sc=sc: E.tensor_tensor(
                    sc[:], xap, psA[:], ALU.mult)), deps=[tx, tscr[b]] + tbc)
                tb_ = P.op("dve", (lambda E, sc=sc, k=k: E.tensor_tensor(
                    out_buf[:, k * 1024:(k + 1) * 1024], sc[:], psB[:], ALU.subtract)), deps=[ta])
                set_reader(ta)
                tscr[b] = tb_
                tw.append(tb_)
            return tw, tbc

        def xdram_stream(bufs, tag, init_dep=None):
            readers = [init_dep or NIL, init_dep or NIL]

            def tile(k):
                b = k % 2
                buf = bufs[b]
                tl = P.dma("act", buf[:], xT[k * 128:(k + 1) * 128, :], deps=[readers[b]], ch=f"{tag}{b}")

                def set_reader(tok, b=b):
                    readers[b] = tok
                return buf[:], tl, set_reader

            return tile

        ln1_src = xdram_stream(xtl, "xl")
        t_xn1, _ = layernorm("ln1", (lambda k: ln1_src(k)), xn)

        def attn_a(t_cast_sync):
            P.phase = "aa"
            t_rt = [P.dma("sp", ropetab[:, i * 1024:(i + 1) * 1024], ropet_i[i],
                          deps=[t_ag1, t_xn1[-1]], ch="rt") for i in range(4)]
            t_rt = [t_rt[-1]]
            cosq, sinq = ropetab[:, 0:1024], ropetab[:, 1024:2048]
            cosk, sink = ropetab[:, 2048:3072], ropetab[:, 3072:4096]
            t_gi = [P.dma("sp", ging_s[:, h * 1024:(h + 1) * 1024], ging_i[h], ch="gi") for h in range(H)]
            t_gi = [t_gi[-1]]
            t_bm = P.dma("sp", binm_s[:], binm_i, ch="bm")

            ps_free = {"A": NIL, "B": NIL}
            wstate = {"i": 0, "rd": [NIL, NIL]}

            def wload(name, m, extra=()):
                b = wstate["i"] % 2
                tl = P.dma("sp", wsl[b][:], slab1(name, m), deps=[wstate["rd"][b], t_ag1] + list(extra), ch=f"w{b}")
                wstate["i"] += 1
                return wsl[b], tl, b

            def proj_full(name, m, ps, ps_key):
                wb, tw, b = wload(name, m)
                tm = None
                for n in range(2):
                    for k in range(KT):
                        deps = [tw] + (t_xn1 if (n == 0 and k == 0) else []) \
                            + ([ps_free[ps_key]] if (n == 0 and k == 0) else [])
                        tm = P.op("pe", (lambda E, wb=wb, k=k, n=n, ps=ps: E.matmul(
                            ps[:, n * 512:(n + 1) * 512], wb[:, k * 128:(k + 1) * 128],
                            xn[:, k * 1024 + n * 512:k * 1024 + (n + 1) * 512],
                            start=(k == 0), stop=(k == KT - 1))), deps=deps)
                wstate["rd"][b] = tm
                return tm

            s0, s1 = sq0, sq1
            s2, s3 = af[:, 8192:9216], af[:, 10240:11264]

            tko = {}
            t_ktok_all = []
            tpsT_free = NIL
            tks_free = [NIL, NIL]
            for h in range(H):
                tm1 = proj_full("wkT", 2 * h, psA, "A")
                tm2 = proj_full("wkT", 2 * h + 1, psB, "B")
                ta = P.op("dve", (lambda E: E.tensor_tensor(s0[:], psA[:], cosk, ALU.mult)), deps=[tm1] + t_rt)
                tb = P.op("dve", (lambda E: E.tensor_tensor(s1[:], psB[:], sink, ALU.mult)), deps=[tm2] + t_rt)
                tr1 = P.op("dve", (lambda E: E.tensor_tensor(kstage[:, 0:1024], s0[:], s1[:], ALU.subtract)),
                           deps=[ta, tb, tks_free[0]])
                tc_ = P.op("dve", (lambda E: E.tensor_tensor(s2[:], psB[:], cosk, ALU.mult)), deps=[tm2])
                td = P.op("dve", (lambda E: E.tensor_tensor(s3[:], psA[:], sink, ALU.mult)), deps=[tm1])
                tr2 = P.op("dve", (lambda E: E.tensor_tensor(kstage[:, 1024:2048], s2[:], s3[:], ALU.add)),
                           deps=[tc_, td, tks_free[1]])
                ps_free["A"] = td
                ps_free["B"] = tc_
                lastev = NIL
                for half, trr in ((0, tr1), (1, tr2)):
                    dt = 2 * h + half
                    for tt in range(8):
                        ttr = P.op("pe", (lambda E, half=half, tt=tt: E.transpose(
                            psT[:, (tt % 4) * 128:((tt % 4) + 1) * 128],
                            kstage[:, half * 1024 + tt * 128:half * 1024 + (tt + 1) * 128], idb[:])),
                            deps=[trr, t_id, tpsT_free])
                        tev = P.op("dve", (lambda E, tt=tt, dt=dt, h=h: E.tensor_scalar(
                            ktok[:, tt * 1024 + dt * 128:tt * 1024 + (dt + 1) * 128],
                            psT[:, (tt % 4) * 128:((tt % 4) + 1) * 128],
                            gkinv_c[:, h:h + 1], None, ALU.mult)), deps=[ttr] + t_cst)
                        tpsT_free = tev
                        t_ktok_all.append(tev)
                        lastev = tev
                    if half == 0:
                        tks_free[0] = lastev
                    else:
                        tks_free[1] = lastev
            t_ms0 = P.op("dve", (lambda E: E.memset(af[:, 12288:16384], 0.0)))
            psE_last = NIL
            msb = af[:, 12288:16384]
            t_msb = {(h, dt): t_ms0 for h in range(H) for dt in range(2)}
            t_vspill = {}
            vst_rd = NIL
            vev_free = [NIL, NIL]
            psV_free = [NIL, NIL, NIL, NIL]
            psV = [(psA, 0), (psA, 512), (psB, 0), (psB, 512)]
            k2rd = [NIL, NIL]
            t_ms_done = []
            for n in range(2):
                tvw = {}
                for m in range(16):
                    wb, twv, b = wload("wvT", m)
                    q = m % 4
                    ps, c0 = psV[q]
                    tm = None
                    for k in range(KT):
                        deps = [twv] + ([psV_free[q]] if k == 0 else [])
                        tm = P.op("pe", (lambda E, wb=wb, k=k, ps=ps, c0=c0, n=n: E.matmul(
                            ps[:, c0:c0 + 512], wb[:, k * 128:(k + 1) * 128],
                            xn[:, k * 1024 + n * 512:k * 1024 + (n + 1) * 512],
                            start=(k == 0), stop=(k == KT - 1))), deps=deps)
                    wstate["rd"][b] = tm
                    vb2 = m % 2
                    tev = P.op("act", (lambda E, ps=ps, c0=c0, vb2=vb2: E.activation(
                        kstage[:, vb2 * 512:vb2 * 512 + 512], ps[:, c0:c0 + 512], AF.Identity)),
                        deps=[tm, vev_free[vb2]])
                    psV_free[q] = tev
                    for j in range(4):
                        ttr = P.op("pe", (lambda E, j=j, vb2=vb2: E.transpose(
                            psT[:, 512 + j * 128:640 + j * 128],
                            kstage[:, vb2 * 512 + j * 128:vb2 * 512 + (j + 1) * 128], idb[:])),
                            deps=[tev, t_id, tpsT_free])
                        tvv = P.op("dve", (lambda E, j=j, m=m: E.tensor_copy(
                            vstage[:, j * 2048 + m * 128:j * 2048 + (m + 1) * 128],
                            psT[:, 512 + j * 128:640 + j * 128])), deps=[ttr, vst_rd])
                        tpsT_free = tvv
                        tvw[(j, m)] = tvv
                    vev_free[vb2] = tvw[(3, m)]
                for j in range(4):
                    tt = n * 4 + j
                    deps_v = [tvw[(j, m)] for m in range(16)]
                    for h in range(H):
                        b2 = (tt * H + h) % 2
                        tk2 = P.op("dve", (lambda E, tt=tt, h=h, b2=b2: E.tensor_scalar(
                            k2b[b2][:, 0:256], ktok[:, tt * 1024 + 2 * h * 128:tt * 1024 + (2 * h + 2) * 128],
                            gkm_c[:, tt * H + h:tt * H + h + 1], None, ALU.mult)),
                            deps=[k2rd[b2]] + t_ktok_all + t_cst)
                        for dt in range(2):
                            tmm_ = P.op("pe", (lambda E, b2=b2, dt=dt, j=j, h=h: E.matmul(
                                psE[:, 0:512], k2b[b2][:, dt * 128:(dt + 1) * 128],
                                vstage[:, j * 2048 + h * 512:j * 2048 + (h + 1) * 512],
                                start=True, stop=True)),
                                deps=[tk2, psE_last] + deps_v + [t_msb[(h, dt)]])
                            tad = P.op("dve", (lambda E, h=h, dt=dt: E.tensor_tensor(
                                msb[:, (2 * h + dt) * 512:(2 * h + dt + 1) * 512],
                                msb[:, (2 * h + dt) * 512:(2 * h + dt + 1) * 512],
                                psE[:, 0:512], ALU.add)), deps=[tmm_, t_msb[(h, dt)]])
                            t_msb[(h, dt)] = tad
                            psE_last = tad
                            k2rd[b2] = tmm_
                    tsp = P.dma("act", vt_d[tt], vstage[:, j * 2048:(j + 1) * 2048],
                                deps=deps_v + [t_msb[(h, dt)] for h in range(H) for dt in range(2)], ch="vsp")
                    t_vspill[tt] = tsp
                vst_rd = tsp
            t_vspF = t_vspill[7]
            t_ms_done = [t_msb[(h, dt)] for h in range(H) for dt in range(2)]
            t_mev = P.dma("act", ccm_in.ap().rearrange("(g p) v -> p g v", p=128),
                          msb.rearrange("p (g v) -> p g v", g=8), deps=t_ms_done, ch="mo")
            t_agm = P.coll((lambda E: E.collective_compute(
                "AllGather", ALU.bypass, replica_groups=[list(range(NCORE))],
                ins=[ccm_in.ap().opt()], outs=[ccm_out.ap().opt()])), deps=[t_mev])

            qst_rd = [NIL, NIL]
            t_qspill = {tt: [] for tt in range(8)}
            for h in range(H):
                tm1 = proj_full("wqT", 2 * h, psA, "A")
                tm2 = proj_full("wqT", 2 * h + 1, psB, "B")
                ging_h = ging_s[:, h * 1024:(h + 1) * 1024]
                ta = P.op("dve", (lambda E: E.tensor_tensor(s0[:], psA[:], cosq, ALU.mult)), deps=[tm1] + t_rt)
                tb = P.op("dve", (lambda E: E.tensor_tensor(s1[:], psB[:], sinq, ALU.mult)), deps=[tm2] + t_rt)
                tr1 = P.op("dve", (lambda E: E.tensor_tensor(s2[:], s0[:], s1[:], ALU.subtract)), deps=[ta, tb])
                tq1 = P.op("dve", (lambda E, ging_h=ging_h: E.tensor_tensor(
                    qstage[:, 0:1024], s2[:], ging_h, ALU.mult)), deps=[tr1, qst_rd[0]] + t_gi)
                tc_ = P.op("dve", (lambda E: E.tensor_tensor(s0[:], psB[:], cosq, ALU.mult)), deps=[tm2, tr1])
                td = P.op("dve", (lambda E: E.tensor_tensor(s1[:], psA[:], sinq, ALU.mult)), deps=[tm1, tr1])
                tr2 = P.op("dve", (lambda E: E.tensor_tensor(s3[:], s0[:], s1[:], ALU.add)), deps=[tc_, td])
                tq2 = P.op("dve", (lambda E, ging_h=ging_h: E.tensor_tensor(
                    qstage[:, 1024:2048], s3[:], ging_h, ALU.mult)), deps=[tr2, qst_rd[1]] + t_gi)
                ps_free["A"] = td
                ps_free["B"] = tc_
                for half, tqq in ((0, tq1), (1, tq2)):
                    dt = 2 * h + half
                    lsp = NIL
                    for tt in range(8):
                        tsp = P.dma("act", qs_d[tt, :, dt * 128:(dt + 1) * 128],
                                    qstage[:, half * 1024 + tt * 128:half * 1024 + (tt + 1) * 128],
                                    deps=[tqq], ch=f"qsp{half}")
                        t_qspill[tt].append(tsp)
                        lsp = tsp
                    qst_rd[half] = lsp

            t_qspF = qst_rd[1]
            t_qspF2 = qst_rd[0]
            t_rope_done = tq2
            t_gw = []
            psg_free = [NIL, NIL]
            for m in range(16):
                wb, twg, b = wload("wgT", m)
                ps = psA if m % 2 == 0 else psB
                pk = m % 2
                tm = None
                for n in range(2):
                    for k in range(KT):
                        deps = [twg] + ([psg_free[pk], ps_free["A"], ps_free["B"]] if (n == 0 and k == 0) else [])
                        tm = P.op("pe", (lambda E, wb=wb, k=k, n=n, ps=ps: E.matmul(
                            ps[:, n * 512:(n + 1) * 512], wb[:, k * 128:(k + 1) * 128],
                            xn[:, k * 1024 + n * 512:k * 1024 + (n + 1) * 512],
                            start=(k == 0), stop=(k == KT - 1))), deps=deps)
                wstate["rd"][b] = tm
                tev = P.op("act", (lambda E, ps=ps, m=m: E.activation(
                    gsilu[:, m * 1024:(m + 1) * 1024], ps[:], AF.Silu)),
                    deps=[tm] + (t_cast_sync if m < 8 else []))
                psg_free[pk] = tev
                t_gw.append(tev)
            t_g_last_mm = tm
            t_xn_done = [t for t in [t_gw[-1]]]
            t_xn_free = t_gw

            t_s0 = P.op("dve", (lambda E: E.memset(ssb[:], 0.0)), deps=t_cast_sync + [t_rope_done])
            tprev = t_s0
            mj_rd = NIL
            for jc in range(NCORE):
                tl = P.dma("act", mjbuf[:].rearrange("p (g v) -> p g v", g=8),
                           ccm_out[jc * 1024:(jc + 1) * 1024, :].rearrange("(g p) v -> p g v", p=128),
                           deps=[t_agm, mj_rd] + t_cast_sync, ch="mj")
                for h in range(H):
                    tprev = P.op("dve", (lambda E, jc=jc, h=h: E.scalar_tensor_tensor(
                        ssb[:, h * 1024:(h + 1) * 1024], mjbuf[:, h * 1024:(h + 1) * 1024],
                        coef_c[:, jc * H + h:jc * H + h + 1], ssb[:, h * 1024:(h + 1) * 1024],
                        ALU.mult, ALU.add)), deps=[tl, tprev] + t_cst)
                mj_rd = tprev
            t_sentry = tprev

            tq_rd = [NIL, NIL]
            tv_rd = [NIL, NIL]
            t_sbf = P.op("dve", (lambda E: E.tensor_copy(sbf[:], ssb[:])),
                         deps=[t_sentry, t_qspF, t_qspF2])
            ps_oT_free = {"oc0": NIL, "oc1": NIL, "T": NIL}
            kf_rd = [t_vspF, t_vspF]
            at_rd = t_vspF
            ob_rd = [t_vspF, t_vspF]
            psE_free = psE_last
            og_w = []
            sq_rd = NIL
            for tt in range(8):
                bq = tt % 2
                tql = P.dma("act", qtile[bq][:], qs_d[tt],
                            deps=[t_qspF, t_qspF2, t_vspF, tq_rd[bq], wstate["rd"][0], wstate["rd"][1]], ch=f"qt{bq}")
                tvl = P.dma("act", vtile[bq][:], vt_d[tt],
                            deps=[t_vspF, tv_rd[bq], wstate["rd"][bq]], ch=f"vt{bq}")
                for cc in range(2):
                    c = tt * 2 + cc
                    base = cc * 64
                    for h in range(H):
                        bkf = h % 2
                        tkt = []
                        for dti in range(2):
                            dt = 2 * h + dti
                            ttr = P.op("pe", (lambda E, tt=tt, dt=dt, base=base: E.transpose(
                                psT[:, 0:64],
                                ktok[base:base + 64, tt * 1024 + dt * 128:tt * 1024 + (dt + 1) * 128],
                                idb[base:base + 64, base:base + 64])), deps=[tpsT_free, t_id] + t_ktok_all)
                            tev = P.op("dve", (lambda E, bkf=bkf, dti=dti: E.tensor_copy(
                                kfsc[bkf][:, dti * 64:(dti + 1) * 64], psT[:, 0:64])),
                                deps=[ttr, kf_rd[bkf]])
                            tpsT_free = tev
                            tkt.append(tev)
                        tsc = None
                        for dti in range(2):
                            dt = 2 * h + dti
                            deps = [tkt[dti], tql] + ([psE_free] if dti == 0 else [])
                            tsc = P.op("pe", (lambda E, bkf=bkf, dti=dti, dt=dt, base=base, bq=bq: E.matmul(
                                psE[base:base + 64, 0:64], kfsc[bkf][:, dti * 64:(dti + 1) * 64],
                                qtile[bq][:, dt * 128 + base:dt * 128 + base + 64],
                                start=(dti == 0), stop=(dti == 1),
                                tile_position=(0, base))), deps=deps)
                        kf_rd[bkf] = tsc
                        tat = P.op("dve", (lambda E, base=base: E.tensor_tensor(
                            atsc[base:base + 64, 0:64], psE[base:base + 64, 0:64],
                            binm_s[base:base + 64, :], ALU.mult)), deps=[tsc, t_bm, at_rd])
                        psE_free = tat
                        okey = "oc%d" % (c % 2)
                        oc0 = (c % 2) * 512
                        to1 = P.op("pe", (lambda E, base=base, oc0=oc0, h=h, bq=bq: E.matmul(
                            psC[0:64, oc0:oc0 + 512], atsc[base:base + 64, 0:64],
                            vtile[bq][base:base + 64, h * 512:(h + 1) * 512],
                            start=True, stop=False,
                            tile_position=(base, 0))), deps=[tat, tvl, ps_oT_free[okey]])
                        to2 = None
                        for dti in range(2):
                            dt = 2 * h + dti
                            to2 = P.op("pe", (lambda E, dt=dt, base=base, oc0=oc0, h=h, dti=dti, bq=bq: E.matmul(
                                psC[0:64, oc0:oc0 + 512], qtile[bq][:, dt * 128 + base:dt * 128 + base + 64],
                                sbf[:, (2 * h + dti) * 512:(2 * h + dti + 1) * 512],
                                start=False, stop=(dti == 1))), deps=[to1, t_sbf])
                        at_rd = to1
                        tT = []
                        for dti in range(2):
                            dt = 2 * h + dti
                            tt_ = P.op("pe", (lambda E, tt=tt, dt=dt, base=base, dti=dti, bq=bq, h=h: E.matmul(
                                psB[:, dti * 512:(dti + 1) * 512],
                                ktok[base:base + 64, tt * 1024 + dt * 128:tt * 1024 + (dt + 1) * 128],
                                vtile[bq][base:base + 64, h * 512:(h + 1) * 512],
                                start=True, stop=True,
                                tile_position=(base, 0))), deps=[tvl, ps_oT_free["T"]] + t_ktok_all)
                            tT.append(tt_)
                        tu1 = P.op("dve", (lambda E, h=h: E.tensor_tensor(
                            ssb[:, h * 1024:(h + 1) * 1024], ssb[:, h * 1024:(h + 1) * 1024],
                            psB[:], ALU.add)), deps=tT + [t_sentry])
                        g64 = float(GAM[h] ** 64)
                        tu2 = P.op("dve", (lambda E, h=h, g64=g64: E.tensor_scalar(
                            ssb[:, h * 1024:(h + 1) * 1024], ssb[:, h * 1024:(h + 1) * 1024],
                            g64, None, ALU.mult)), deps=[tu1])
                        ps_oT_free["T"] = tu1
                        tsq_ = P.op("act", (lambda E, oc0=oc0: E.activation(
                            sq0[0:64, 0:512], psC[0:64, oc0:oc0 + 512], AF.Square,
                            accum_out=smb[0:64, 0:1])), deps=[to2, sq_rd])
                        tvar = P.op("dve", (lambda E: E.tensor_scalar(
                            smb[0:64, 1:2], smb[0:64, 0:1], 1.0 / 512.0, 1e-5, ALU.mult, ALU.add)), deps=[tsq_])
                        tsqr = P.op("act", (lambda E: E.activation(
                            smb[0:64, 1:2], smb[0:64, 1:2], AF.Sqrt)), deps=[tvar])
                        trec = P.op("dve", (lambda E: E.reciprocal(smb[0:64, 2:3], smb[0:64, 1:2])), deps=[tsqr])
                        bo = c % 2
                        tob = P.op("act", (lambda E, oc0=oc0, bo=bo: E.activation(
                            obsc[bo][0:64, :], psC[0:64, oc0:oc0 + 512], AF.Copy,
                            scale=smb[0:64, 2:3])), deps=[trec, ob_rd[bo]])
                        sq_rd = tsq_
                        tgn = P.op("dve", (lambda E, bo=bo: E.tensor_tensor(
                            obsc[bo][0:64, :], obsc[bo][0:64, :], gnormw_c[0:64, :], ALU.mult)),
                            deps=[tob] + t_cst)
                        ps_oT_free["oc%d" % (c % 2)] = tob
                        for j in range(4):
                            ttr = P.op("pe", (lambda E, j=j, bo=bo: E.transpose(
                                psT[:, 128:192],
                                obsc[bo][0:64, j * 128:(j + 1) * 128], idb[0:64, 0:64])),
                                deps=[tgn, t_id, tpsT_free])
                            m_og = h * 4 + j
                            tgw = P.op("dve", (lambda E, m_og=m_og, c=c: E.tensor_tensor(
                                og[:, m_og * 1024 + c * 64:m_og * 1024 + (c + 1) * 64],
                                psT[:, 128:192],
                                gsilu[:, m_og * 1024 + c * 64:m_og * 1024 + (c + 1) * 64], ALU.mult)),
                                deps=[ttr] + t_gw + t_xn1)
                            tpsT_free = tgw
                            og_w.append(tgw)
                        ob_rd[bo] = tgw
                    t_sbf = P.op("dve", (lambda E: E.tensor_copy(sbf[:], ssb[:])),
                                 deps=[tu2, t_sbf, to2])
                tq_rd[bq] = to2
                tv_rd[bq] = tt_
            t_chunk_pe = tt_
            wstate["rd"] = [t_chunk_pe, t_chunk_pe]

            xs = xdram_stream(xto, "xo", init_dep=t_sbf)
            t_x1 = []
            psg_free2 = [NIL, NIL]
            for m in range(16):
                wb, two, b = wload("woT", m)
                ps = psA if m % 2 == 0 else psB
                pk = m % 2
                tm = None
                for n in range(2):
                    for k in range(KT):
                        deps = [two] + og_w + ([psg_free2[pk], ps_free["A"], ps_free["B"], psg_free[0], psg_free[1]] if (n == 0 and k == 0) else [])
                        tm = P.op("pe", (lambda E, wb=wb, k=k, n=n, ps=ps: E.matmul(
                            ps[:, n * 512:(n + 1) * 512], wb[:, k * 128:(k + 1) * 128],
                            og[:, k * 1024 + n * 512:k * 1024 + (n + 1) * 512],
                            start=(k == 0), stop=(k == KT - 1))), deps=deps)
                wstate["rd"][b] = tm
                xap, tx, set_reader = xs(m)
                tev = P.op("dve", (lambda E, ps=ps, m=m, xap=xap: E.tensor_tensor(
                    xsb[:, m * 1024:(m + 1) * 1024], xap, ps[:], ALU.add)),
                    deps=[tm, tx, t_sentry, mj_rd, t_sbf] + t_rt)
                set_reader(tev)
                psg_free2[pk] = tev
                t_x1.append(tev)
            return t_x1

        t_x = attn_a(cast1 + cast2 + [t_rwb])

        fwbcB = [ab[:, 54784:55808], ab[:, 52992:54016]]
        fact = ab[:, 16384:30720]
        fgsl = [ab[:, 40960:43008], ab[:, 43008:45056]]
        fusl = [ab[:, 45056:47104], ab[:, 47104:49152]]
        fdsl = [ab[:, 49152:50944], ab[:, 50944:52736]]
        mjtmp = af[:, 17408:18432]
        wrow = [af[0:1, 16384:16896], af[0:1, 16896:17408]]

        def ffn(tag, xsrc, war_all, n_exp=E):
            tw_ln, t_lnbc = layernorm(
                "fln",
                (lambda k: (xsb[:, k * 1024:(k + 1) * 1024], xsrc[k], (lambda tok: None))),
                xn, sq_war=war_all)
            P.phase = "ff"
            twl = tw_ln[-1]
            if not FFN_ROUTER:
                xev0 = {}
                for m in range(KT):
                    for n in range(2):
                        xev0[(m, n)] = tw_ln[m]
                return xev0, twl, [war_all, war_all]
            lgc = sm[:, 0:128]
            wfs = sm[:, 128:256]
            scr = sm[:, 256:384]
            psE_fr = [twl, twl]
            t_wT = []
            psTr_fr = twl
            for tb in range(8):
                pc = (tb % 2) * 16
                tm = None
                for k in range(KT):
                    deps = [t_rwb, tw_ln[k]] + ([psE_fr[tb % 2]] if k == 0 else [])
                    tm = P.op("pe", (lambda E, tb=tb, k=k, pc=pc: E.matmul(
                        psE[:, pc:pc + 16], xn[:, k * 1024 + tb * 128:k * 1024 + (tb + 1) * 128],
                        rwb[:, k * 16:(k + 1) * 16], start=(k == 0), stop=(k == KT - 1))), deps=deps)
                lT = lgc[:, tb * 16:(tb + 1) * 16]
                t0 = scr[:, tb * 16:(tb + 1) * 16]
                ge = wfs[:, tb * 16:(tb + 1) * 16]
                mxa = sm[:, 416 + tb:417 + tb]
                mx4 = sm[:, 384 + tb:385 + tb]
                neg = sm[:, 392 + tb:393 + tb]
                zz = sm[:, 400 + tb:401 + tb]
                rz = sm[:, 408 + tb:409 + tb]
                te = P.op("dve", (lambda E, lT=lT, pc=pc: E.tensor_tensor(lT, psE[:, pc:pc + 16], ebias_c, ALU.add)),
                          deps=[tm, twl] + t_cst)
                psE_fr[tb % 2] = te
                tq = P.op("dve", (lambda E, mxa=mxa, lT=lT: E.tensor_reduce(mxa, lT, AX.X, ALU.max)), deps=[te])
                tg = P.op("dve", (lambda E, t0=t0, lT=lT, mxa=mxa: E.tensor_scalar(t0, lT, mxa, None, ALU.is_ge)), deps=[tq])
                ts_ = P.op("dve", (lambda E, t0=t0, lT=lT: E.scalar_tensor_tensor(t0, t0, -1e5, lT, ALU.mult, ALU.add)), deps=[tg])
                for _ in range(2):
                    tq = P.op("dve", (lambda E, mxa=mxa, t0=t0: E.tensor_reduce(mxa, t0, AX.X, ALU.max)), deps=[ts_])
                    tg = P.op("dve", (lambda E, ge=ge, t0=t0, mxa=mxa: E.tensor_scalar(ge, t0, mxa, None, ALU.is_ge)), deps=[tq])
                    ts_ = P.op("dve", (lambda E, t0=t0, ge=ge: E.scalar_tensor_tensor(t0, ge, -1e5, t0, ALU.mult, ALU.add)), deps=[tg])
                t4 = P.op("dve", (lambda E, mx4=mx4, t0=t0: E.tensor_reduce(mx4, t0, AX.X, ALU.max)), deps=[ts_])
                t5 = P.op("dve", (lambda E, neg=neg, mx4=mx4: E.tensor_scalar(neg, mx4, -1.0, None, ALU.mult)), deps=[t4])
                t6 = P.op("act", (lambda E, t0=t0, lT=lT, neg=neg: E.activation(t0, lT, AF.Exp, bias=neg)), deps=[t5, ts_])
                t7 = P.op("dve", (lambda E, ge=ge, lT=lT, mx4=mx4: E.tensor_scalar(ge, lT, mx4, None, ALU.is_ge)), deps=[t4, ts_])
                t8 = P.op("dve", (lambda E, t0=t0, ge=ge: E.tensor_tensor(t0, t0, ge, ALU.mult)), deps=[t6, t7])
                t9 = P.op("dve", (lambda E, zz=zz, t0=t0: E.tensor_reduce(zz, t0, AX.X, ALU.add)), deps=[t8])
                t10 = P.op("dve", (lambda E, rz=rz, zz=zz: E.reciprocal(rz, zz)), deps=[t9])
                t11 = P.op("dve", (lambda E, ge=ge, t0=t0, rz=rz: E.tensor_scalar(ge, t0, rz, None, ALU.mult)), deps=[t10])
                t12 = P.op("pe", (lambda E, ge=ge: E.matmul(
                    psC[0:16, 0:128], ge, idf[:], start=True, stop=True)),
                    deps=[t11, t_idf, psTr_fr])
                t13 = P.op("dve", (lambda E, tb=tb: E.tensor_copy(sm2[0:16, tb * 128:(tb + 1) * 128], psC[0:16, 0:128])),
                           deps=[t12, twl] + t_lnbc)
                psTr_fr = t13
                t_wT.append(t13)

            def mk_loader(bufs, eng, chtag, ncol, slab_fn, t_ag):
                st = {"i": 0, "rd": [war_all, war_all]}

                def load(e, m, extra=()):
                    b = st["i"] % 2
                    st["i"] += 1
                    tl = P.dma(eng, bufs[b][:, 0:ncol], slab_fn(e, m),
                               deps=[st["rd"][b], t_ag] + list(extra), ch=f"{chtag}{b}")
                    return bufs[b], tl, b
                return load, st
            gload, gst = mk_loader(fgsl, "sp", "fg", 2048, slab2g, t_ag2a)
            uload, ust = mk_loader(fusl, "pool", "fu", 2048, slab2u, t_ag2a)
            dload, dst_ = mk_loader(fdsl, "act", "fd", 1792, slab2d, t_ag2b)

            S = {"psE": [psE_fr[0], psE_fr[1], psTr_fr]}
            row_rd = [twl, twl]
            fw_rd = [war_all, war_all]

            def fwbc_build(e):
                eb2 = e % 2
                toks = []
                for n2 in range(2):
                    tx1 = P.op("pe", (lambda E, e=e, n2=n2: E.matmul(
                        psE[0:1, 0:512], idf[0:16, e:e + 1], sm2[0:16, n2 * 512:(n2 + 1) * 512],
                        start=True, stop=True)), deps=S["psE"] + t_wT + [t_idf])
                    tx2 = P.op("dve", (lambda E, n2=n2: E.tensor_copy(wrow[n2], psE[0:1, 0:512])),
                               deps=[tx1, row_rd[n2]])
                    tx3 = P.op("pe", (lambda E, n2=n2: E.matmul(
                        psE[:, 0:512], onesrf[:], wrow[n2], start=True, stop=True)),
                        deps=[tx2, t_onesr])
                    row_rd[n2] = tx3
                    tx4 = P.op("act", (lambda E, eb2=eb2, n2=n2: E.activation(
                        fwbcB[eb2][:, n2 * 512:(n2 + 1) * 512], psE[:, 0:512], AF.Copy)),
                        deps=[tx3, fw_rd[eb2]])
                    S["psE"] = [tx4]
                    toks.append(tx4)
                return toks

            psA_fr = [twl, twl]
            psB_fr = [twl, twl]
            psC_fr = [twl, twl]
            mjtmp_rd = [twl, twl]
            fact_rd = {(i, n): t_ag2b for i in range(IM) for n in range(2)}
            prev_down = [twl, twl]
            xev = {}
            for m in range(KT):
                for n in range(2):
                    xev[(m, n)] = tw_ln[m]
            pool_last = [war_all, war_all]
            tdm = twl
            for e in range(n_exp):
                wb_toks = fwbc_build(e)
                pool_tok = {}
                for i in range(IM):
                    gb, tgl, bg = gload(e, i)
                    ub, tul, bu = uload(e, i)
                    tmg_last = tmu_last = None
                    for n in range(2):
                        tmg = None
                        for k in range(KT):
                            deps = [tgl] + ([psA_fr[n]] if k == 0 else []) \
                                + (tw_ln if (e == 0 and i == 0 and k == 0) else [])
                            tmg = P.op("pe", (lambda E, gb=gb, k=k, n=n: E.matmul(
                                psA[:, n * 512:(n + 1) * 512], gb[:, k * 128:(k + 1) * 128],
                                xn[:, k * 1024 + n * 512:k * 1024 + (n + 1) * 512],
                                start=(k == 0), stop=(k == KT - 1))), deps=deps)
                        tmu = None
                        for k in range(KT):
                            deps = [tul] + ([psB_fr[n]] if k == 0 else [])
                            tmu = P.op("pe", (lambda E, ub=ub, k=k, n=n: E.matmul(
                                psB[:, n * 512:(n + 1) * 512], ub[:, k * 128:(k + 1) * 128],
                                xn[:, k * 1024 + n * 512:k * 1024 + (n + 1) * 512],
                                start=(k == 0), stop=(k == KT - 1))), deps=deps)
                        tsil = P.op("act", (lambda E, n=n: E.activation(
                            mjtmp[:, n * 512:(n + 1) * 512], psA[:, n * 512:(n + 1) * 512], AF.Silu)),
                            deps=[tmg, mjtmp_rd[n]])
                        psA_fr[n] = tsil
                        tmul = P.op("dve", (lambda E, i=i, n=n: E.tensor_tensor(
                            fact[:, i * 1024 + n * 512:i * 1024 + (n + 1) * 512],
                            mjtmp[:, n * 512:(n + 1) * 512], psB[:, n * 512:(n + 1) * 512], ALU.mult)),
                            deps=[tsil, tmu, fact_rd[(i, n)], prev_down[n]])
                        psB_fr[n] = tmul
                        mjtmp_rd[n] = tmul
                        tscale = P.op("pool", (lambda E, i=i, n=n, eb2=e % 2: E.tensor_tensor(
                            fact[:, i * 1024 + n * 512:i * 1024 + (n + 1) * 512],
                            fact[:, i * 1024 + n * 512:i * 1024 + (n + 1) * 512],
                            fwbcB[eb2][:, n * 512:(n + 1) * 512], ALU.mult)),
                            deps=[tmul] + wb_toks)
                        pool_tok[(i, n)] = tscale
                        fact_rd[(i, n)] = tscale
                        tmg_last, tmu_last = tmg, tmu
                    gst["rd"][bg] = tmg_last
                    ust["rd"][bu] = tmu_last
                pool_last[e % 2] = pool_tok[(IM - 1, 1)]
                fw_rd[e % 2] = pool_tok[(IM - 1, 1)]
                for m in range(KT):
                    db, tdl, bd = dload(e, m)
                    for n in range(2):
                        tdm = None
                        for i in range(IM):
                            deps = [tdl, pool_tok[(i, n)]] + ([psC_fr[n]] if i == 0 else [])
                            tdm = P.op("pe", (lambda E, db=db, i=i, n=n: E.matmul(
                                psC[:, n * 512:(n + 1) * 512], db[:, i * 128:(i + 1) * 128],
                                fact[:, i * 1024 + n * 512:i * 1024 + (n + 1) * 512],
                                start=(i == 0), stop=(i == IM - 1))), deps=deps)
                        tev = P.op("dve", (lambda E, m=m, n=n: E.tensor_tensor(
                            xsb[:, m * 1024 + n * 512:m * 1024 + (n + 1) * 512],
                            xsb[:, m * 1024 + n * 512:m * 1024 + (n + 1) * 512],
                            psC[:, n * 512:(n + 1) * 512], ALU.add)),
                            deps=[tdm, xev[(m, n)]])
                        psC_fr[n] = tev
                        xev[(m, n)] = tev
                    dst_["rd"][bd] = tdm
                prev_down = [tdm, tdm]
            return xev, tdm, pool_last

        MULT = 1.0 / float(np.sqrt(128.0))

        def attn_b(xsrc, war_dve, war_pool):
            kb_sb = ab[:, 0:16384]
            vb_sb = ab[:, 16384:32896]
            xnr = ab[:, 32896:49280]
            wqb = [ab[:, 49280:51328], ab[:, 51328:53376]]
            qtb = [ab[:, 53376:54400], ab[:, 54400:55424]]
            pP = [ab[:, 55808:56320], ab[:, 56320:56832]]
            ystage = ab[:, 56832:57344]
            vstg = [ab[:, 57344:58368], ab[:, 58368:59392]]
            tri_s = ab[:, 59392:59520]

            tw3, t_ln3bc = layernorm(
                "bln",
                (lambda k: (xsb[:, k * 1024:(k + 1) * 1024], xsrc[k], (lambda tok: None))),
                xn, sq_war=war_dve)
            P.phase = "ab"
            twl = tw3[-1]
            t_xno = P.dma("pool", ccxn_in.ap().rearrange("(m p) t -> p m t", p=128),
                          xn.rearrange("p (m t) -> p m t", m=16), deps=tw3 + [war_pool], ch="xno")
            t_agx = P.coll((lambda E: E.collective_compute(
                "AllGather", ALU.bypass, replica_groups=[list(range(NCORE))],
                ins=[ccxn_in.ap().opt()], outs=[ccxn_out.ap().opt()])), deps=[t_xno])
            t_tri = P.dma("sp", tri_s[:], tri_i, deps=[war_dve], ch="tri")

            wq_rd = [war_dve, war_dve]
            xnr_rd = [war_dve, war_dve]
            vstg_rd = [war_dve, war_dve]
            psum_fr = {0: twl, 1: twl}
            t_kevac = {}
            t_vb = {}
            t_qspill = {}
            psT_fr = war_dve
            t_ydma = []
            for r in range(NCORE):
                t_xl = []
                for half in range(2):
                    tl = P.dma("sp", xnr[:, half * 8192:(half + 1) * 8192].rearrange("p (k t) -> p k t", k=8),
                               ccxn_out[r * D + half * 1024:r * D + (half + 1) * 1024, :]
                               .rearrange("(k p) t -> p k t", p=128),
                               deps=[t_agx, xnr_rd[half]], ch=f"xn{half}")
                    t_xl.append(tl)
                for m in range(6):
                    b = m % 2
                    twq = P.dma("act", wqb[b][:], wqkv_bf[m * 128:(m + 1) * 128, :],
                                deps=[wq_rd[b]] + cast2, ch=f"wq{b}")
                    ps = psA if m % 2 == 0 else psB
                    pk = m % 2
                    tm = None
                    for n in range(2):
                        for k in range(KT):
                            deps = [twq, t_xl[k // 8]] + ([psum_fr[pk]] if (n == 0 and k == 0) else [])
                            tm = P.op("pe", (lambda E, b=b, k=k, n=n, ps=ps: E.matmul(
                                ps[:, n * 512:(n + 1) * 512], wqb[b][:, k * 128:(k + 1) * 128],
                                xnr[:, k * 1024 + n * 512:k * 1024 + (n + 1) * 512],
                                start=(k == 0), stop=(k == KT - 1))), deps=deps)
                    wq_rd[b] = tm
                    if m >= 4:
                        xnr_rd = [tm, tm]
                    h = m % 2
                    if m < 2:
                        bs = 0
                        tev = P.op("act", (lambda E, ps=ps, h=h: E.activation(
                            vstg[0][:], ps[:], AF.Identity, bias=bqkv_c[:, h:h + 1])),
                            deps=[tm, vstg_rd[0]] + t_cst)
                        psum_fr[pk] = tev
                        tsp = P.dma("pool", qb_d.ap()[r * 8:(r + 1) * 8, :, h * 128:(h + 1) * 128]
                                    .rearrange("j p q -> p j q"),
                                    vstg[0].rearrange("p (j q) -> p j q", j=8),
                                    deps=[tev], ch="qsp")
                        t_qspill[(r, h)] = tsp
                        vstg_rd[0] = tsp
                    elif m < 4:
                        tev = P.op("act", (lambda E, ps=ps, h=h, r=r, m=m: E.activation(
                            kb_sb[:, h * 8192 + r * 1024:h * 8192 + (r + 1) * 1024], ps[:],
                            AF.Identity, bias=bqkv_c[:, m:m + 1])),
                            deps=[tm, t_agx] + t_cst)
                        psum_fr[pk] = tev
                        t_kevac[(h, r)] = tev
                    else:
                        tev = P.op("act", (lambda E, ps=ps, m=m: E.activation(
                            vstg[1][:], ps[:], AF.Identity, bias=bqkv_c[:, m:m + 1])),
                            deps=[tm, vstg_rd[1]] + t_cst)
                        psum_fr[pk] = tev
                        for j in range(8):
                            kbg = r * 8 + j
                            ttr = P.op("pe", (lambda E, j=j: E.transpose(
                                psT[:, 0:128], vstg[1][:, j * 128:(j + 1) * 128], idb[:])),
                                deps=[tev, t_id, psT_fr])
                            tve = P.op("dve", (lambda E, h=h, kbg=kbg: E.tensor_copy(
                                vb_sb[:, (h * 64 + kbg) * 129:(h * 64 + kbg) * 129 + 128],
                                psT[:, 0:128])), deps=[ttr, war_dve])
                            psT_fr = tve
                            tme = P.op("pool", (lambda E, h=h, kbg=kbg: E.memset(
                                vb_sb[:, (h * 64 + kbg) * 129 + 128:(h * 64 + kbg) * 129 + 129], 1.0)),
                                deps=[tve, war_pool])
                            t_vb[(h, kbg)] = tme
                        vstg_rd[1] = ttr

            psA_fr2 = [t_qspill[(7, 1)], t_qspill[(7, 1)]]
            qtb_rd = [war_dve, war_dve]
            pnum = [psC, psC, psB, psB]
            SUBC = [0, 512, 0, 512]
            pnum_fr = {s: psum_fr[s % 2] for s in range(4)}
            ytr_rd = t_qspill[(7, 0)]
            yst_rd = [twl] * 4
            smb_rd = twl
            tri_rd = t_tri
            for qB in range(16):
                bq = qB % 2
                r0 = qB // 2
                tql = P.dma("pool", qtb[bq].rearrange("p (h j q) -> p h j q", h=2, j=4),
                            qb_d.ap()[qB * 4:(qB + 1) * 4, :, :].rearrange("j p (h q) -> p h j q", h=2),
                            deps=[t_qspill[(r0, 0)], t_qspill[(r0, 1)], qtb_rd[bq]], ch=f"qt{bq}")
                for h in range(2):
                    texp = None
                    for kb in range(4 * qB + 4):
                        b = kb % 2
                        rk = kb // 8
                        tsc = P.op("pe", (lambda E, b=b, h=h, kb=kb, bq=bq: E.matmul(
                            psA[:, b * 512:(b + 1) * 512],
                            kb_sb[:, h * 8192 + kb * 128:h * 8192 + (kb + 1) * 128],
                            qtb[bq][:, h * 512:(h + 1) * 512],
                            start=True, stop=True)),
                            deps=[tql, t_kevac[(h, rk)], psA_fr2[b]])
                        j = kb - 4 * qB
                        if j >= 0:
                            toks = []
                            if j > 0:
                                tz = P.op("pool", (lambda E, b=b, j=j: E.memset(
                                    pP[b][:, 0:j * 128], 0.0)), deps=[tsc, war_pool])
                                toks.append(tz)
                            tdg = P.op("dve", (lambda E, b=b, j=j: E.tensor_tensor(
                                psA[:, b * 512 + j * 128:b * 512 + (j + 1) * 128],
                                psA[:, b * 512 + j * 128:b * 512 + (j + 1) * 128],
                                tri_s[:], ALU.add)), deps=[tsc, tri_rd])
                            texp = P.op("act", (lambda E, b=b, j=j: E.activation(
                                pP[b][:, j * 128:512], psA[:, b * 512 + j * 128:b * 512 + 512],
                                AF.Exp, scale=MULT)), deps=[tdg] + toks)
                            tri_rd = texp
                        else:
                            texp = P.op("act", (lambda E, b=b: E.activation(
                                pP[b][:], psA[:, b * 512:(b + 1) * 512], AF.Exp, scale=MULT)),
                                deps=[tsc])
                        psA_fr2[b] = texp
                        for s in range(4):
                            tnm = P.op("pe", (lambda E, b=b, h=h, kb=kb, s=s, qB=qB: E.matmul(
                                pnum[s][:, SUBC[s]:SUBC[s] + 129],
                                pP[b][:, s * 128:(s + 1) * 128],
                                vb_sb[:, (h * 64 + kb) * 129:(h * 64 + kb + 1) * 129],
                                start=(kb == 0), stop=(kb == 4 * qB + 3))),
                                deps=[texp, t_vb[(h, kb)]] + ([pnum_fr[s]] if kb == 0 else []))
                            if kb == 4 * qB + 3:
                                pnum_fr[s] = tnm
                    tyes = []
                    for s in range(4):
                        rc = smb[:, h * 4 + s:h * 4 + s + 1]
                        trc = P.op("dve", (lambda E, h=h, s=s, rc=rc: E.reciprocal(
                            rc, pnum[s][:, SUBC[s] + 128:SUBC[s] + 129])),
                            deps=[pnum_fr[s], smb_rd])
                        tya = P.op("act", (lambda E, h=h, s=s, rc=rc: E.activation(
                            ystage[:, s * 128:(s + 1) * 128], pnum[s][:, SUBC[s]:SUBC[s] + 128],
                            AF.Copy, scale=rc)), deps=[trc, yst_rd[s]])
                        pnum_fr[s] = tya
                        ttr = P.op("pe", (lambda E, s=s: E.transpose(
                            psT[:, 0:128], ystage[:, s * 128:(s + 1) * 128], idb[:])),
                            deps=[tya, t_id, psT_fr])
                        tye = P.op("dve", (lambda E, h=h, s=s: E.tensor_copy(
                            vstg[0][:, h * 512 + s * 128:h * 512 + (s + 1) * 128], psT[:, 0:128])),
                            deps=[ttr, ytr_rd])
                        psT_fr = tye
                        yst_rd[s] = ttr
                        tyes.append(tye)
                    smb_rd = tyes[0]
                    tyd = P.dma("act", ccy_in[h * 128:(h + 1) * 128, qB * 512:(qB + 1) * 512],
                                vstg[0][:, h * 512:(h + 1) * 512], deps=tyes, ch="yo")
                    ytr_rd = tyd
                    t_ydma.append(tyd)
                qtb_rd[bq] = tnm
            t_agy = P.coll((lambda E: E.collective_compute(
                "AllGather", ALU.bypass, replica_groups=[list(range(NCORE))],
                ins=[ccy_in.ap().opt()], outs=[ccy_out.ap().opt()])), deps=t_ydma)

            P.phase = "abo"
            ybuf = ab[:, 0:16384]
            t_yl = {}
            CIDV = {}
            for k in range(KT):
                def mk_yload(k=k):
                    def fn(E):
                        if "v" not in CIDV:
                            CIDV["v"] = nc.values_load(
                                cid_sb[0:1, 0:1], engines=[mybir.EngineType.Pool],
                                min_val=0, max_val=7, skip_runtime_bounds_check=True)
                        return E.dma_start(ybuf[:, k * 1024:(k + 1) * 1024],
                                           ccy_out[k * 128:(k + 1) * 128, bass.ts(CIDV["v"], TC)])
                    return fn
                tl = P.op("pool", mk_yload(), deps=[t_agy, t_cid, tnm], dma=True, ch=f"yl{k % 2}")
                t_yl[k] = tl
            t_ylF = [t_yl[KT - 2], t_yl[KT - 1]]
            xevb = []
            psum_fr2 = [pnum_fr[3], pnum_fr[3]]
            last = None
            for m in range(KT):
                b = m % 2
                twp = P.dma("act", wqb[b][:], slab1("wprojT", m),
                            deps=[wq_rd[b], t_ag1], ch=f"wq{b}")
                ps = psA if m % 2 == 0 else psB
                pk = m % 2
                tm = None
                for n in range(2):
                    for k in range(KT):
                        deps = [twp, t_ylF[k % 2]] + ([psum_fr2[pk], psA_fr2[0], psA_fr2[1]] if (n == 0 and k == 0) else [])
                        tm = P.op("pe", (lambda E, b=b, k=k, n=n, ps=ps: E.matmul(
                            ps[:, n * 512:(n + 1) * 512], wqb[b][:, k * 128:(k + 1) * 128],
                            ybuf[:, k * 1024 + n * 512:k * 1024 + (n + 1) * 512],
                            start=(k == 0), stop=(k == KT - 1))), deps=deps)
                wq_rd[b] = tm
                tev = P.op("dve", (lambda E, ps=ps, m=m: E.scalar_tensor_tensor(
                    xsb[:, m * 1024:(m + 1) * 1024], ps[:], bproj_c[:, m:m + 1],
                    xsb[:, m * 1024:(m + 1) * 1024], ALU.add, ALU.add)),
                    deps=[tm, xsrc[m]] + t_cst)
                psum_fr2[pk] = tev
                xevb.append(tev)
                last = tev
            return xevb, last

        if n_sub >= 2:
            xev1, t_f1_pe, t_f1_pool = ffn("f1", t_x, t_x[-1], n_exp=FFN_EXPERTS)
            xw = [xev1[(m, 1)] for m in range(KT)] + [xev1[(m, 0)] for m in range(KT)]
        else:
            xw = t_x
        if n_sub >= 3:
            xsrc3 = [xev1[(m, 1)] for m in range(KT)]
            xevb, t_b_last = attn_b(xsrc3, xev1[(KT - 1, 1)], t_f1_pool[0])
            xw = xevb
        if n_sub >= 4:
            xev2, t_f2_pe, t_f2_pool = ffn("f2", xevb, t_b_last, n_exp=FFN_EXPERTS)
            xw = [xev2[(m, 1)] for m in range(KT)] + [xev2[(m, 0)] for m in range(KT)]
        P.phase = "out"
        t_out = P.dma("sp", out_t.rearrange("(m p) t -> p m t", p=128),
                      xsb.rearrange("p (m t) -> p m t", m=16), deps=xw, ch="out")
        P.wait("sp", t_out)

        print(f"[build] ops={P.n_ops} sems={len(P.sems)}")
        P.finalize()
    return nc



def _prep_inputs(inputs):
    g1 = _manifest()
    off1, n1 = _offsets(g1)
    x = np.asarray(inputs["L0_attn_a_x"])[0]
    xf = np.ascontiguousarray(x.T)

    blob1 = np.empty(n1, dtype=np.float32)
    for (name, m, cols), src in zip(
            g1, [np.asarray(inputs["L0_attn_a_Wq"]).T, np.asarray(inputs["L0_attn_a_Wk"]).T,
                 np.asarray(inputs["L0_attn_a_Wv"]).T, np.asarray(inputs["L0_attn_a_Wg"]).T,
                 np.asarray(inputs["L0_attn_a_Wo"]).T, np.asarray(inputs["L0_attn_b_w_proj"])]):
        o, mm_, cols_ = off1[name]
        blob1[o:o + mm_ * 128 * cols_] = _pretile(np.ascontiguousarray(src)).ravel()
    gate = np.asarray(inputs["L0_ffn_gate_proj"])
    up = np.asarray(inputs["L0_ffn_up_proj"])
    down = np.asarray(inputs["L0_ffn_down_proj"])
    blob2a = np.empty((E * 2 * IM * 128, D), dtype=np.float32)
    blob2b = np.empty((E * 16 * 128, FI), dtype=np.float32)
    for e in range(E):
        blob2a[(e * 2 * IM) * 128:(e * 2 * IM + IM) * 128] = _pretile(gate[e]).reshape(IM * 128, D)
        blob2a[(e * 2 * IM + IM) * 128:(e * 2 * IM + 2 * IM) * 128] = _pretile(up[e]).reshape(IM * 128, D)
        blob2b[(e * 16) * 128:(e + 1) * 16 * 128] = _pretile(down[e]).reshape(16 * 128, FI)

    w_qkv = np.asarray(inputs["L0_attn_b_w_qkv"])
    b_qkv = np.asarray(inputs["L0_attn_b_b_qkv"])
    b_proj = np.asarray(inputs["L0_attn_b_b_proj"])
    rw = np.asarray(inputs["L0_ffn_router_w"])
    ebias = np.asarray(inputs["L0_ffn_expert_bias"])
    gnw = np.asarray(inputs["L0_attn_a_gnorm_w"])

    ging, gkinv, gkm, binm = _decay_tables()
    tri = _tri_mask()
    shared = dict(
        ging=ging, gkinv=gkinv, gkm=gkm, binm=binm, tri=tri,
        identb=np.eye(128, dtype=bf), identf=np.eye(128, dtype=np.float32),
        onesf=np.ones((128, 1), np.float32), onesrowf=np.ones((1, 128), np.float32),
        rwT=np.ascontiguousarray(rw.T),
        ebias=np.ascontiguousarray(np.broadcast_to(ebias[None, :], (128, 16))),
        gnormw=np.ascontiguousarray(np.broadcast_to(gnw[None, :], (128, DV))),
        bproj=np.ascontiguousarray(b_proj.reshape(16, 128).T),
    )
    in_maps = []
    sh1 = n1 // NCORE
    s2a = blob2a.shape[0] // NCORE
    s2b = blob2b.shape[0] // NCORE
    for r in range(NCORE):
        cols = []
        for part, base in ((0, 0), (1, D), (2, 2 * D)):
            cols.extend(range(base + 2 * r * 128, base + (2 * r + 2) * 128))
        wslice = np.ascontiguousarray(w_qkv[:, cols])
        bslice = b_qkv[cols]
        m = dict(shared)
        m.update(
            xT=np.ascontiguousarray(xf[:, r * TC:(r + 1) * TC]),
            wsh1=blob1[r * sh1:(r + 1) * sh1].reshape(-1, 1024),
            wsh2a=np.ascontiguousarray(blob2a[r * s2a:(r + 1) * s2a]),
            wsh2b=np.ascontiguousarray(blob2b[r * s2b:(r + 1) * s2b]),
            wqkvT=_pretile(wslice).reshape(6, 128, D),
            bqkv=np.ascontiguousarray(bslice.reshape(6, 128).T),
            ropet=_rope_tables(r),
            coef=_coef_table(r),
            cid=np.array([[r]], np.int32),
        )
        in_maps.append(m)
    return in_maps


_NC_CACHE = {}


def kernel(**inputs):
    in_maps = _prep_inputs(inputs)
    key = N_SUB
    if key not in _NC_CACHE:
        _NC_CACHE[key] = build_nc(N_SUB)
    nc = _NC_CACHE[key]
    res = run_bass_kernel_spmd(nc, in_maps, core_ids=list(range(NCORE)))
    outs = [np.asarray(res.results[r]["out"]) for r in range(NCORE)]
    x = np.concatenate([o.T for o in outs], axis=0)
    return x[None, :, :]
